# version 17
# baseline (speedup 1.0000x reference)
import numpy as np
import concourse.bass as bass
import concourse.mybir as mybir
from concourse.bass_utils import run_bass_kernel_spmd
from contextlib import ExitStack

F32 = mybir.dt.float32
BF16 = mybir.dt.bfloat16
FP16 = mybir.dt.float16
ALU = mybir.AluOpType
AF = mybir.ActivationFunctionType
AX = mybir.AxisListType

NCORES = 8
N, D, I = 2048, 1024, 2816
NT = N // 128
KC = D // 128
IC = I // 128
RND_HC = 6291456.0
VELT_C = 4194305.0
PE_TAPS = list(range(1, 21))
PAD = 32
QW = 1024


class Sched:

    ENGS = ("pe", "dve", "act", "pool", "sp")
    NPOOL = {"act": 12, "pool": 8, "sp": 20}

    def __init__(self, nc):
        self.nc = nc
        self.items = {e: [] for e in self.ENGS}
        self.count = {}
        self.sems = {}
        self.rot = {e: 0 for e in self.NPOOL}

    def make_sems(self, es):
        for e in self.ENGS:
            self.sems[e] = es.enter_context(self.nc.semaphore(f"sem_{e}"))
            self.count[e] = 0
        for e, np_ in self.NPOOL.items():
            for i in range(np_):
                q = f"{e}q{i}"
                self.sems[q] = es.enter_context(self.nc.semaphore(f"sem_{q}"))
                self.count[q] = 0

    def op(self, eng, fn, deps=()):
        self.count[eng] += 1
        tok = (eng, self.count[eng])
        self.items[eng].append(
            (fn, tuple(d for d in deps if d is not None), tok, 1, eng))
        return tok

    def dma(self, eng, out, in_, deps=(), **kw):
        assert eng in self.NPOOL
        q = f"{eng}q{self.rot[eng]}"
        self.rot[eng] = (self.rot[eng] + 1) % self.NPOOL[eng]
        prev = self.count[q]
        deps = tuple(d for d in deps if d is not None)
        if prev > 0:
            deps = deps + ((q, prev),)
        self.count[q] += 16
        tok = (q, self.count[q])
        self.items[eng].append(
            (lambda e: e.dma_start(out=out, in_=in_, **kw), deps, tok, 16, q))
        return tok

    def finalize(self):
        self.prior = {}
        for eng in self.ENGS:
            run = {}
            for (fn, deps, tok, inc, sem) in self.items[eng]:
                self.prior[tok] = dict(run)
                run[sem] = tok[1]

    def replay(self, eng, e):
        waited = {}
        for (fn, deps, tok, inc, sem) in self.items[eng]:
            want = {}
            for (s, v) in deps:
                if s == eng:
                    v = min(v, self.prior[tok].get(s, 0))
                if v > want.get(s, 0):
                    want[s] = v
            for s, v in want.items():
                if v > waited.get(s, 0):
                    e.wait_ge(self.sems[s], v)
                    waited[s] = v
            inst = fn(e)
            if inst is not None:
                inst.then_inc(self.sems[sem], inc)

    def all_dma_barrier(self, eng):
        return tuple((f"{eng}q{i}", self.count[f"{eng}q{i}"])
                     for i in range(self.NPOOL[eng])
                     if self.count[f"{eng}q{i}"] > 0)


class Arena:
    def __init__(self, nc, total_bytes):
        self.nc = nc
        self.total = total_bytes
        self.handle = None
        self.named = []

    def reserve(self, es):
        self.handle = es.enter_context(
            self.nc.sbuf_tensor("arena", [128, self.total], mybir.dt.uint8))

    def at(self, name, offset, shape, dtype):
        per_part = int(np.prod(shape[1:])) * mybir.dt.size(dtype)
        assert offset % 32 == 0, (name, offset)
        assert offset + per_part <= self.total, (name, offset, per_part)
        h = self.nc.virtual_sbuf_tensor(name, shape, dtype)
        self.named.append((h, offset))
        return h

    def finalize(self):
        base = self.nc.lookup_mloc(self.handle).addr
        for h, off in self.named:
            self.nc._tensor(h.name, h.shape, h.dtype, type="SB")
            mloc = self.nc.lookup_mloc(h)
            mloc.allocated = not self.nc.target_bir_lowering
            mloc.addr = base + off


MAP_P = {"u": "pool", "w": "pool", "hi": "pool", "G1": "pool", "q": "pool"}
MAP_D = {"u": "pool", "w": "dve", "hi": "dve", "G1": "pool", "q": "dve"}


def qv_stages(sched, bufs, src_ap, q_out_ap, width, src_tok,
              extra_q_deps, prev1, prev2, emap, T):
    nb = width // 16
    amax, s_t, rs_t = bufs["amax"], bufs["s"], bufs["rs"]
    u, cu, w = bufs["u"], bufs["cu"], bufs["w"]
    hi, Ra, Rb, lcp = bufs["hi"], bufs["Ra"], bufs["Rb"], bufs["lcp"]
    p1 = prev1 or {}
    p2 = prev2 or {}

    def g(d, *keys):
        return tuple(d[k] for k in keys if d.get(k) is not None)

    src_deps = tuple(src_tok) if isinstance(src_tok, (tuple, list)) and \
        src_tok and isinstance(src_tok[0], tuple) else (src_tok,)
    src3 = src_ap.rearrange("p (b k) -> p b k", k=16)
    rs_b = rs_t[:, :nb].unsqueeze(2).broadcast_to([128, nb, 16])
    s_b = s_t[:, :nb].unsqueeze(2).broadcast_to([128, nb, 16])
    u3 = u[:, :width].rearrange("p (b k) -> p b k", k=16)
    uf, cuf, wf = u[:, :width], cu[:, :width], w[:, :width]
    hif, raf, rbf, lcpf = (hi[:, :width], Ra[:, :width], Rb[:, :width],
                           lcp[:, :width])
    ctok = tuple(bufs.get("ctok") or ())
    cC = bufs["cC"][:, 0:1]
    cH = bufs["cH"][:, 0:1]
    cn1 = bufs["cn1"][:, 0:1]
    q3 = q_out_ap.rearrange("p (b k) -> p b k", k=16)
    hi3 = hif.rearrange("p (b k) -> p b k", k=16)

    def st_am():
        T["am"] = sched.op("dve", lambda e: e.tensor_reduce(
            amax[:, :nb], src3, AX.X, ALU.max, apply_absolute_value=True),
            deps=src_deps + g(p2, "s"))

    def st_s():
        T["s"] = sched.op("dve", lambda e: e.tensor_scalar(
            s_t[:, :nb], amax[:, :nb], 1.0 / 6.0, 1e-12, ALU.mult, ALU.max),
            deps=(T["am"],) + g(p2, "q", "rs"))

    def st_rs():
        T["rs"] = sched.op("dve", lambda e: e.reciprocal(
            rs_t[:, :nb], s_t[:, :nb]), deps=(T["s"],) + g(p2, "u"))

    def st_u():
        T["u"] = sched.op(emap["u"], lambda e: e.tensor_tensor(
            u3, src3, rs_b, ALU.mult), deps=(T["rs"],) + g(p1, "b5p"))

    def st_cu():
        T["cu"] = sched.op("act", lambda e: e.activation(
            cuf, uf, AF.Identity, scale=cC),
            deps=(T["u"],) + g(p1, "lcp") + ctok)

    def st_w():
        T["w"] = sched.op(emap["w"], lambda e: e.tensor_tensor(
            wf, cuf, uf, ALU.subtract), deps=(T["cu"],) + g(p1, "G"))

    def st_hi():
        T["hi"] = sched.op(emap["hi"], lambda e: e.tensor_tensor(
            hif, cuf, wf, ALU.subtract), deps=(T["w"],) + g(p1, "q"))

    def st_b5p():
        T["b5p"] = sched.op("act", lambda e: e.activation(
            cuf, uf, AF.Identity, bias=cH), deps=(T["hi"],))

    def st_Ra():
        T["Ra"] = sched.op("act", lambda e: e.activation(
            raf, hif, AF.Relu, bias=cn1), deps=(T["hi"],))

    def st_Rb():
        T["Rb"] = sched.op("act", lambda e: e.activation(
            rbf, hif, AF.Relu, bias=cn1, scale=cn1), deps=(T["Ra"],))

    def st_lcp():
        T["lcp"] = sched.op("dve", lambda e: e.tensor_scalar(
            lcpf, cuf, RND_HC, -1.0, ALU.subtract, ALU.max),
            deps=(T["b5p"],))

    def st_G1():
        T["G1"] = sched.op(emap["G1"], lambda e: e.tensor_tensor(
            raf, raf, rbf, ALU.subtract), deps=(T["Rb"],))

    def st_G():
        T["G"] = sched.op("dve", lambda e: e.scalar_tensor_tensor(
            hif, lcpf, 1.0, raf, ALU.min, ALU.add),
            deps=(T["lcp"], T["G1"]))

    def st_q():
        T["q"] = sched.op(emap["q"], lambda e: e.tensor_tensor(
            q3, hi3, s_b, ALU.mult), deps=(T["G"],) + tuple(extra_q_deps))

    return [st_am, st_s, st_rs, st_u, st_cu, st_w, st_hi, st_b5p,
            st_Ra, st_Rb, st_lcp, st_G1, st_G, st_q]


def emit_quant_v(sched, bufs, src_ap, q_out_ap, width, src_tok,
                 extra_q_deps=(), prev1=None, prev2=None, q_on_dve=False,
                 emap=None):
    T = {}
    emap = emap or (dict(MAP_P, q="dve") if q_on_dve else MAP_P)
    for st in qv_stages(sched, bufs, src_ap, q_out_ap, width, src_tok,
                        extra_q_deps, prev1, prev2, emap, T):
        st()
    return T


def emit_quant_pair(sched, specs):
    Ts = [dict() for _ in specs]
    plans = [qv_stages(sched, b, s, q, wd, tok, ex, p1, p1, em, Ts[i])
             for i, (b, s, q, wd, tok, ex, p1, em) in enumerate(specs)]
    for si in range(len(plans[0])):
        for p in plans:
            p[si]()
    return Ts


def build_fused():
    nc = bass.Bass("TRN2", target_bir_lowering=False, debug=False)
    x_d = nc.dram_tensor("x", [N, D], F32, kind="ExternalInput").ap()
    w1_d = nc.dram_tensor("w1", [D, D], F32, kind="ExternalInput").ap()
    w2_d = nc.dram_tensor("w2", [D, D], F32, kind="ExternalInput").ap()
    bias_d = nc.dram_tensor("bias", [1, D], F32, kind="ExternalInput").ap()
    et_d = nc.dram_tensor("et", [21, D], F32, kind="ExternalInput").ap()
    id_d = nc.dram_tensor("ident", [21, 21], F32, kind="ExternalInput").ap()
    eye_d = nc.dram_tensor("eye128", [128, 128], F32, kind="ExternalInput").ap()
    wg_d = nc.dram_tensor("wg", [I, D], F32, kind="ExternalInput").ap()
    wu_d = nc.dram_tensor("wu", [I, D], F32, kind="ExternalInput").ap()
    wd_d = nc.dram_tensor("wd", [D, I], F32, kind="ExternalInput").ap()
    out_d = nc.dram_tensor("out", [N, D], F32, kind="ExternalOutput").ap()

    xb_dram = nc.dram_tensor("xb_dram", [N, D], FP16, kind="Internal").ap()
    y1_dram = nc.dram_tensor("y1_dram", [N, D], F32, kind="Internal").ap()
    y1q_dram = nc.dram_tensor("y1q_dram", [N, D], BF16, kind="Internal").ap()
    hq_dram = nc.dram_tensor("hq_dram", [N, I], BF16, kind="Internal").ap()
    wgq_d = nc.dram_tensor("wgq", [I, D], BF16, kind="Internal").ap()
    wuq_d = nc.dram_tensor("wuq", [I, D], BF16, kind="Internal").ap()
    wdq_d = nc.dram_tensor("wdq", [D, I], BF16, kind="Internal").ap()

    es = ExitStack()
    sched = Sched(nc)
    sched.make_sems(es)

    AR = 207 * 1024
    ar = Arena(nc, AR)
    ar.reserve(es)

    y1qT = ar.at("y1qT", 0, [128, KC * N], BF16)
    convA = ar.at("convA", 0, [128, N], FP16)
    convB = ar.at("convB", 4096, [128, N], FP16)
    convC = ar.at("convC", 8192, [128, N], FP16)
    convD = ar.at("convD", 12288, [128, N], FP16)
    convE = ar.at("convE", 16384, [128, N], FP16)
    et_rows = ar.at("et_rows", 20480, [21, D], F32)
    ident21 = ar.at("ident21", 24576, [21, 21], F32)

    GEN = 32768
    x_res = ar.at("x_res", GEN, [128, NT * D], F32)
    xT = ar.at("xT", GEN + 65536, [128, KC * (N + PAD)], FP16)
    hT = ar.at("hT", GEN + 98816, [128, KC * N], FP16)
    W1b = ar.at("W1b", GEN + 131584, [128, KC * D], FP16)
    W2b = ar.at("W2b", GEN + 148096, [128, KC * D], FP16)

    def mk_qset(tag, W, u_o, cu_o, w_o, hi_o, ra_o, rb_o, am_o, s_o, rs_o,
                src_o=None, q_o=None):
        d = {"u": ar.at(f"{tag}_u", u_o, [128, W], F32),
             "cu": ar.at(f"{tag}_cu", cu_o, [128, W], F32),
             "w": ar.at(f"{tag}_w", w_o, [128, W], F32),
             "lcp": ar.at(f"{tag}_lcp", w_o, [128, W], BF16),
             "hi": ar.at(f"{tag}_hi", hi_o, [128, W], BF16),
             "Ra": ar.at(f"{tag}_Ra", ra_o, [128, W], BF16),
             "Rb": ar.at(f"{tag}_Rb", rb_o, [128, W], BF16),
             "amax": ar.at(f"{tag}_am", am_o, [128, W // 16], F32),
             "s": ar.at(f"{tag}_s", s_o, [128, W // 16], F32),
             "rs": ar.at(f"{tag}_rs", rs_o, [128, W // 16], F32)}
        if src_o is not None:
            d["src"] = ar.at(f"{tag}_src", src_o, [128, W], F32)
        if q_o is not None:
            d["qout"] = ar.at(f"{tag}_q", q_o, [128, W], BF16)
        return d

    vw2 = [mk_qset("vw0", 1024, 49152, 53248, 57344, 61440, 63488, 65536,
                   69632, 69888, 70144, src_o=45056, q_o=67584),
           mk_qset("vw1", 1024, 74496, 78592, 82688, 86784, 88832, 90880,
                   94976, 95232, 95488, src_o=70400, q_o=92928)]

    vy1 = {
        "u": ar.at("vy_u", 197248, [128, QW], F32),
        "cu": ar.at("vy_cu", 201344, [128, QW], F32),
        "w": ar.at("vy_w", 16384, [128, QW], F32),
        "lcp": ar.at("vy_lcp", 16384, [128, QW], BF16),
        "hi": ar.at("vy_hi", 2048, [128, QW], BF16),
        "Ra": ar.at("vy_Ra", 6144, [128, QW], BF16),
        "Rb": ar.at("vy_Rb", 10240, [128, QW], BF16),
        "amax": ar.at("vy_amax", 205440, [128, QW // 16], F32),
        "s": ar.at("vy_s", 205728, [128, QW // 16], F32),
        "rs": ar.at("vy_rs", 206016, [128, QW // 16], F32),
    }
    y1q_st = [ar.at("y1qs0", 26624, [128, D], BF16),
              ar.at("y1qs1", 30720, [128, D], BF16)]
    diag16 = ar.at("diag16", 197248, [128, len(PE_TAPS) * 128], FP16)

    wgT = ar.at("wgT", 32768, [128, KC * I], BF16)
    wuT = ar.at("wuT", 77824, [128, KC * I], BF16)
    vh2 = [mk_qset("vh0", 1408, 122880, 134144, 151040, 145408, 175776,
                   181408, 162304, 163040, 163776),
           mk_qset("vh1", 1408, 128512, 139776, 156672, 148224, 178592,
                   184224, 162656, 163392, 164128)]
    h_t = ar.at("h_t", 164512, [128, I], F32)
    hq_t = ar.at("hq_t", 187040, [128, I], BF16)
    wd_src = ar.at("wd_src", 192672, [128, 1408], F32)
    sg_b = [None] * 4
    up_b = [None] * 4
    cur2 = 199456
    for i in range(4):
        sg_b[i] = ar.at(f"sg{i}", cur2, [128, 512], BF16); cur2 += 1024
        up_b[i] = ar.at(f"up{i}", cur2, [128, 512], BF16); cur2 += 1024
    assert cur2 <= 208160, cur2

    hqT = ar.at("hqT", 32768, [128, IC * N], BF16)
    wdT = ar.at("wdT", 122880, [128, IC * D], BF16)
    y1r = ar.at("y1r", 167936, [128, 3 * D], F32)
    outb = ar.at("outb", 180224, [128, 3 * D], F32)

    sm = 208160
    et_fm = ar.at("et_fm", sm, [128, KC * 21], F32); sm += KC * 21 * 4
    sm = (sm + 31) // 32 * 32
    biasb = ar.at("biasb", sm, [1, D], BF16); sm += D * 2
    onesb = ar.at("onesb", sm, [1, 128], BF16); sm += 256
    epsc = ar.at("epsc", sm, [128, 1], F32); sm += 32
    mvst = ar.at("mvst", sm, [128, 8], F32); sm += 32
    bnbuf = ar.at("bnbuf", sm, [128, 12], F32); sm += 64
    rstdc = ar.at("rstdc", sm, [128, 1], F32); sm += 32
    sqc = ar.at("sqc", sm, [128, 1], F32); sm += 32
    nmr = ar.at("nmr", sm, [128, 1], F32); sm += 32
    eye16 = ar.at("eye16", sm, [128, 128], FP16); sm += 256
    cvelt = ar.at("cvelt", sm, [128, 1], F32); sm += 32
    crndhc = ar.at("crndhc", sm, [128, 1], F32); sm += 32
    cneg1 = ar.at("cneg1", sm, [128, 1], F32); sm += 32
    assert sm <= AR, sm

    att_ps = [es.enter_context(nc.psum_tensor(f"aps{i}", [128, 1024], F32))
              for i in range(4)]

    ar.finalize()

    t_ep = sched.op("dve", lambda e: e.memset(epsc[:], 1e-5))
    t_ones = sched.op("dve", lambda e: e.memset(onesb[:], 1.0))
    ctok = (sched.op("dve", lambda e: e.memset(cvelt[:], VELT_C)),
            sched.op("dve", lambda e: e.memset(crndhc[:], RND_HC)),
            sched.op("dve", lambda e: e.memset(cneg1[:], -1.0)))
    consts = {"cC": cvelt, "cH": crndhc, "cn1": cneg1, "ctok": ctok}
    for b in vw2 + vh2 + [vy1]:
        b.update(consts)
    t_pad = [sched.op("dve", lambda e, k=k: e.memset(
        xT[:, k * (N + PAD): k * (N + PAD) + PAD], 0.0)) for k in range(KC)]

    t_xres = [None] * NT
    for t in range(3):
        t_xres[t] = sched.dma("sp", x_res[:, t * D:(t + 1) * D],
                              x_d[t * 128:(t + 1) * 128, :])

    wjobs = []
    for src_d, dst_d in ((wg_d, wgq_d), (wu_d, wuq_d)):
        for j in range(IC):
            wjobs.append((src_d[j * 128:(j + 1) * 128, :],
                          dst_d[j * 128:(j + 1) * 128, :]))
    NWJ = len(wjobs)
    wload = [None] * NWJ
    wstore = [None] * NWJ
    wq_prev = [None, None]
    wload[0] = sched.dma("sp", vw2[0]["src"][:, :], wjobs[0][0])
    wload[1] = sched.dma("sp", vw2[1]["src"][:, :], wjobs[1][0])

    def emit_wpair(j):
        specs = []
        for o, em in ((0, MAP_P), (1, MAP_D)):
            p = (j + o) % 2
            S = vw2[p]
            ex = (wstore[j + o - 2],) if j + o >= 2 else ()
            specs.append((S, S["src"][:, :], S["qout"][:, :], 1024,
                          (wload[j + o],), ex, wq_prev[p], em))
        Ts = emit_quant_pair(sched, specs)
        for o in (0, 1):
            p = (j + o) % 2
            wq_prev[p] = Ts[o]
            wstore[j + o] = sched.dma("sp", wjobs[j + o][1],
                                      vw2[p]["qout"][:, :],
                                      deps=(Ts[o]["q"],))
            if j + o + 2 < NWJ:
                wload[j + o + 2] = sched.dma("sp", vw2[p]["src"][:, :],
                                             wjobs[j + o + 2][0],
                                             deps=(Ts[o]["u"],))

    t_xb = sched.dma("pool", xb_dram[:], x_d[:])
    t_xT = [sched.dma("sp", xT[:, k * (N + PAD) + PAD:(k + 1) * (N + PAD)],
                      xb_dram[:, k * 128:(k + 1) * 128],
                      deps=(t_xb, t_pad[k]), transpose=True) for k in range(KC)]
    t_w1 = [sched.dma("pool", W1b[:, k * D:(k + 1) * D],
                      w1_d[k * 128:(k + 1) * 128, :]) for k in range(KC)]
    t_w2 = [sched.dma("pool", W2b[:, k * D:(k + 1) * D],
                      w2_d[k * 128:(k + 1) * 128, :]) for k in range(KC)]
    t_bias = sched.dma("pool", biasb[:], bias_d[:])
    t_eye = sched.dma("pool", eye16[:], eye_d[:])
    t_etl = sched.dma("sp", et_rows[:], et_d[:])
    t_id = [sched.dma("sp", ident21[:], id_d[:])]
    t_etfm = []
    for k in range(KC):
        t_tr = sched.op("pe", lambda e, k=k: e.transpose(
            att_ps[0][:, 0:21], et_rows[0:21, k * 128:(k + 1) * 128], ident21[:]),
            deps=tuple(t_id) + (t_etl, t_etfm[-1] if t_etfm else None))
        t_cp = sched.op("dve", lambda e, k=k: e.tensor_copy(
            et_fm[:, k * 21:(k + 1) * 21], att_ps[0][:, 0:21]), deps=(t_tr,))
        t_etfm.append(t_cp)
    t_et_done = t_etfm[-1]

    def xsh(k, j):
        base = k * (N + PAD) + PAD - j
        return xT[:, base: base + N]

    def etc(k, j):
        return et_fm[:, k * 21 + j: k * 21 + j + 1]

    conv_done = []
    conv_ps = [att_ps[3][:, 0:512], att_ps[3][:, 512:1024]]
    cp_guard = [None, None]
    cpar = [0]
    wj_next = [0]
    for k in range(KC):
        npe = len(PE_TAPS)
        eye_b = eye16[:].unsqueeze(1).broadcast_to([128, npe, 128])
        etv = et_fm[:, k * 21 + PE_TAPS[0]: k * 21 + PE_TAPS[0] + npe]
        et_b = etv.unsqueeze(2).broadcast_to([128, npe, 128])
        t_dg = sched.op("dve", lambda e, k=k, eye_b=eye_b, et_b=et_b: e.tensor_tensor(
            diag16[:].rearrange("p (j m) -> p j m", m=128), eye_b, et_b,
            ALU.mult), deps=(t_et_done, t_eye) +
            (tuple(conv_done[-1]) if conv_done else ()))
        pe_evacs = []
        for nt in range(4):
            par = cpar[0] % 2
            cpar[0] += 1

            def mk_conv(e, k=k, nt=nt, par=par):
                last = None
                for ji, j in enumerate(PE_TAPS):
                    e.ldweights(diag16[:, ji * 128:(ji + 1) * 128])
                    last = e.matmul(
                        conv_ps[par], diag16[:, ji * 128:(ji + 1) * 128],
                        xT[:, k * (N + PAD) + PAD - j + nt * 512:
                           k * (N + PAD) + PAD - j + nt * 512 + 512],
                        start=(ji == 0), stop=(ji == len(PE_TAPS) - 1))
                return last
            cdeps = (t_dg, t_xT[k])
            if cp_guard[par] is not None:
                cdeps = cdeps + (cp_guard[par],)
            t_cmm = sched.op("pe", mk_conv, deps=cdeps)
            t_evc = sched.op("act", lambda e, k=k, nt=nt, par=par: e.activation(
                hT[:, k * N + nt * 512: k * N + (nt + 1) * 512],
                conv_ps[par], AF.Identity), deps=(t_cmm,))
            cp_guard[par] = t_evc
            pe_evacs.append(t_evc)
        conv_done.append(tuple(pe_evacs))
        for _ in range(3):
            if wj_next[0] < NWJ:
                emit_wpair(wj_next[0])
                wj_next[0] += 2
    conv_done = [t for evs in conv_done for t in evs]
    assert wj_next[0] == NWJ
    for t in range(3, NT):
        t_xres[t] = sched.dma("sp", x_res[:, t * D:(t + 1) * D],
                              x_d[t * 128:(t + 1) * 128, :],
                              deps=(wstore[NWJ - 1], wstore[NWJ - 2]))

    def mk_mm(e, t, ps):
        last = None
        for k in range(KC):
            stat = xT[:, k * (N + PAD) + PAD + t * 128:
                      k * (N + PAD) + PAD + (t + 1) * 128]
            e.ldweights(stat)
            e.matmul(ps[:, 0:512], stat, W1b[:, k * D: k * D + 512],
                     start=(k == 0), stop=False)
            last = e.matmul(ps[:, 512:1024], stat,
                            W1b[:, k * D + 512:(k + 1) * D],
                            start=(k == 0), stop=False)
        return last

    def mk_mm2(e, t, ps):
        for k in range(KC):
            stat = hT[:, k * N + t * 128: k * N + (t + 1) * 128]
            e.ldweights(stat)
            e.matmul(ps[:, 0:512], stat, W2b[:, k * D: k * D + 512],
                     start=False, stop=False)
            e.matmul(ps[:, 512:1024], stat, W2b[:, k * D + 512:(k + 1) * D],
                     start=False, stop=False)
        e.ldweights(onesb[0:1, :])
        e.matmul(ps[:, 0:512], onesb[0:1, :], biasb[0:1, 0:512],
                 start=False, stop=True)
        return e.matmul(ps[:, 512:1024], onesb[0:1, :], biasb[0:1, 512:1024],
                        start=False, stop=True)

    attn_evac = [None] * NT
    tok_x = [None] * NT
    t_y1d = [None] * NT
    ln_prev = None
    y1q_tok = [None] * NT
    y1q_dump = [None] * NT
    qprev = {}
    prev_dump = {0: None, 1: None}

    for t in range(3):
        tok_x[t] = sched.op("pe", lambda e, t=t: mk_mm(e, t, att_ps[t % 3]),
                            deps=tuple(t_xT) + tuple(t_w1) + (t_et_done,))
    for t in range(NT):
        ps = att_ps[t % 3]
        tok_h = sched.op("pe", lambda e, t=t, ps=ps: mk_mm2(e, t, ps),
                         deps=tuple(conv_done) + tuple(t_w2) +
                         (tok_x[t], t_ones, t_bias))
        t_ev = sched.op("dve", lambda e, t=t, ps=ps: e.tensor_tensor(
            x_res[:, t * D:(t + 1) * D], ps[:, :], x_res[:, t * D:(t + 1) * D],
            ALU.add), deps=(tok_h, t_xres[t]))
        attn_evac[t] = t_ev
        if t + 3 < NT:
            tok_x[t + 3] = sched.op(
                "pe", lambda e, t=t: mk_mm(e, t + 3, att_ps[(t + 3) % 3]),
                deps=(t_ev,))
        t_bn0 = sched.op("dve", lambda e, t=t: e.bn_stats(
            bnbuf[:, 0:6], x_res[:, t * D: t * D + 512]), deps=(t_ev, ln_prev))
        t_bn = sched.op("dve", lambda e, t=t: e.bn_stats(
            bnbuf[:, 6:12], x_res[:, t * D + 512:(t + 1) * D]), deps=(t_bn0,))
        t_ag = sched.op("dve", lambda e: e.bn_aggr(mvst[:, 0:2], bnbuf[:]),
                        deps=(t_bn,))
        t_sq = sched.op("act", lambda e: e.activation(
            sqc[:], mvst[:, 1:2], AF.Sqrt, bias=epsc[:, 0:1]), deps=(t_ag, t_ep))
        t_rsd = sched.op("dve", lambda e: e.reciprocal(rstdc[:], sqc[:]),
                         deps=(t_sq,))
        t_nm0 = sched.op("dve", lambda e: e.tensor_scalar(
            nmr[:], mvst[:, 0:1], rstdc[:, 0:1], -1.0, ALU.mult, ALU.mult),
            deps=(t_rsd,))
        t_nm = sched.op("act", lambda e, t=t: e.activation(
            x_res[:, t * D:(t + 1) * D], x_res[:, t * D:(t + 1) * D],
            AF.Identity, bias=nmr[:, 0:1], scale=rstdc[:, 0:1]),
            deps=(t_nm0, t_sq))
        ln_prev = t_nm
        t_y1d[t] = sched.dma("sp", y1_dram[t * 128:(t + 1) * 128, :],
                             x_res[:, t * D:(t + 1) * D], deps=(t_nm,))
        stg = y1q_st[t % 2]
        qdeps = ((prev_dump[t % 2],) if prev_dump[t % 2] else ())
        Tq = emit_quant_v(sched, vy1, x_res[:, t * D:(t + 1) * D],
                          stg[:], QW, (t_nm,), extra_q_deps=qdeps,
                          prev1=qprev.get(t - 1), prev2=qprev.get(t - 1))
        qprev[t] = Tq
        y1q_tok[t] = Tq["q"]
        y1q_dump[t] = sched.dma("sp", y1q_dram[t * 128:(t + 1) * 128, :],
                                stg[:], deps=(Tq["q"],))
        prev_dump[t % 2] = y1q_dump[t]

    y1qT_half = [None, None]
    all_dumps = tuple(y1q_dump)
    for half in range(2):
        y1qT_half[half] = [
            sched.dma("sp", y1qT[:, k * N + half * 1024: k * N + half * 1024 + 1024],
                      y1q_dram[half * 1024:(half + 1) * 1024,
                               k * 128:(k + 1) * 128],
                      deps=all_dumps,
                      transpose=True) for k in range(KC)]

    def xdead(byte0, byte1):
        t0 = max(0, (byte0 - 32768) // 4096)
        t1 = min(NT - 1, (byte1 - 1 - 32768) // 4096)
        dd = []
        for tt in range(t0, t1 + 1):
            dd += [y1q_tok[tt], t_y1d[tt]]
        return tuple(dd)
    wsts = tuple(wstore)
    t_wgT = [sched.dma("sp", wgT[:, k * I:(k + 1) * I],
                       wgq_d[:, k * 128:(k + 1) * 128], transpose=True,
                       deps=xdead(32768 + k * 5632, 32768 + (k + 1) * 5632) +
                       wsts)
             for k in range(KC)]
    t_wuT = [sched.dma("sp", wuT[:, k * I:(k + 1) * I],
                       wuq_d[:, k * 128:(k + 1) * 128], transpose=True,
                       deps=xdead(77824 + k * 5632, 77824 + (k + 1) * 5632) +
                       ((attn_evac[NT - 1],) if k >= 4 else ()) + wsts)
             for k in range(KC)]

    IBANKS = [(ib * 512, min(512, I - ib * 512)) for ib in range((I + 511) // 512)]
    t_hq_store = [None] * NT
    wd_store = [None] * 16
    psum_guard = {}
    h_guard = [()]
    range_prev = [None, None]
    chain2 = [None, None]
    tog = [0]
    h_parts_of = [None] * NT
    pslot = [0]

    def emit_ffn_tile(t):
        h_parts = []
        for bi, (i0, w) in enumerate(IBANKS):
            ps = att_ps[pslot[0] % 4]
            sg = sg_b[pslot[0] % 4]
            uv = up_b[pslot[0] % 4]
            deps = tuple(t_wgT) + tuple(t_wuT) + tuple(y1qT_half[t // 8])
            if psum_guard.get(pslot[0] % 4) is not None:
                deps = deps + (psum_guard[pslot[0] % 4],)
            pslot[0] += 1

            def mk_gmm(e, t=t, i0=i0, w=w, ps=ps):
                last = None
                for k in range(KC):
                    stat = y1qT[:, k * N + t * 128: k * N + (t + 1) * 128]
                    e.ldweights(stat)
                    e.matmul(ps[:, 0:w], stat, wgT[:, k * I + i0: k * I + i0 + w],
                             start=(k == 0), stop=(k == KC - 1))
                    last = e.matmul(ps[:, 512:512 + w], stat,
                                    wuT[:, k * I + i0: k * I + i0 + w],
                                    start=(k == 0), stop=(k == KC - 1))
                return last
            t_mm = sched.op("pe", mk_gmm, deps=deps)
            t_sg = sched.op("act", lambda e, sg=sg, ps=ps, w=w: e.activation(
                sg[:, 0:w], ps[:, 0:w], AF.Silu), deps=(t_mm,))
            t_uv = sched.op("act", lambda e, uv=uv, ps=ps, w=w: e.activation(
                uv[:, 0:w], ps[:, 512:512 + w], AF.Identity), deps=(t_sg,))
            t_hm = sched.op("dve",
                            lambda e, sg=sg, uv=uv, i0=i0, w=w:
                            e.tensor_tensor(h_t[:, i0:i0 + w], sg[:, 0:w],
                                            uv[:, 0:w], ALU.mult),
                            deps=(t_uv,) + h_guard[0])
            psum_guard[(pslot[0] - 1) % 4] = t_hm
            h_parts.append(t_hm)
        h_parts_of[t] = h_parts

    def emit_hquant(t):
        specs = []
        ps = []
        for half, em in ((0, MAP_P), (1, MAP_D)):
            p = tog[0] % 2
            tog[0] += 1
            ps.append(p)
            S = vh2[p]
            c0 = half * 1408
            ex = (range_prev[half],) if range_prev[half] else ()
            specs.append((S, h_t[:, c0:c0 + 1408], hq_t[:, c0:c0 + 1408],
                          1408, tuple(h_parts_of[t]), ex, chain2[p], em))
        Ts = emit_quant_pair(sched, specs)
        for half in range(2):
            chain2[ps[half]] = Ts[half]
        h_guard[0] = (Ts[0]["u"], Ts[1]["u"])
        t_st = sched.dma("sp", hq_dram[t * 128:(t + 1) * 128, :],
                         hq_t[0:128, :], deps=(Ts[0]["q"], Ts[1]["q"]))
        t_hq_store[t] = t_st
        range_prev[0] = t_st
        range_prev[1] = t_st

    wd_jobs = []
    for i in range(8):
        for c in (0, 1408):
            wd_jobs.append((wd_d[i * 128:(i + 1) * 128, c:c + 1408],
                            wdq_d[i * 128:(i + 1) * 128, c:c + 1408]))
    wd_load = [None] * 16
    wd_T = [None] * 16

    def emit_wdjob(i):
        src_ap, dst_ap = wd_jobs[i]
        ldeps = (y1q_tok[NT - 1],) if i == 0 else (wd_T[i - 1]["u"],)
        wd_load[i] = sched.dma("sp", wd_src[:, :], src_ap, deps=ldeps)
        p = tog[0] % 2
        tog[0] += 1
        S = vh2[p]
        Tq = emit_quant_v(sched, S, wd_src[:, :], hq_t[:, 0:1408], 1408,
                          (wd_load[i],),
                          extra_q_deps=((range_prev[0],)
                                        if range_prev[0] else ()),
                          prev1=chain2[p], prev2=chain2[p],
                          emap=(MAP_D if i % 2 else MAP_P))
        chain2[p] = Tq
        wd_T[i] = Tq
        wd_store[i] = sched.dma("sp", dst_ap, hq_t[:, 0:1408], deps=(Tq["q"],))
        range_prev[0] = wd_store[i]

    wd_next = [0]
    for t in range(NT):
        emit_ffn_tile(t)
        emit_hquant(t)
        if t >= 8:
            for _ in range(2):
                if wd_next[0] < 16:
                    emit_wdjob(wd_next[0])
                    wd_next[0] += 1
    assert wd_next[0] == 16

    wdsts = tuple(wd_store)
    t_wdT = [sched.dma("sp", wdT[:, ic * D:(ic + 1) * D],
                       wdq_d[:, ic * 128:(ic + 1) * 128], transpose=True,
                       deps=tuple(t_hq_store) + wdsts) for ic in range(IC)]
    t_hqT = []
    for half in range(2):
        t_hqT.append([sched.dma(
            "sp", hqT[:, ic * N + half * 1024: ic * N + half * 1024 + 1024],
            hq_dram[half * 1024:(half + 1) * 1024, ic * 128:(ic + 1) * 128],
            transpose=True,
            deps=tuple(tk for tk in t_hq_store[half * 8:(half + 1) * 8] if tk) +
            (wd_store[14], wd_store[15]))
            for ic in range(IC)])

    def mk_dmm(e, t, ps, ic0, ic1):
        last = None
        for ic in range(ic0, ic1):
            stat = hqT[:, ic * N + t * 128: ic * N + (t + 1) * 128]
            e.ldweights(stat)
            e.matmul(ps[:, 0:512], stat, wdT[:, ic * D: ic * D + 512],
                     start=(ic == 0), stop=(ic == IC - 1))
            last = e.matmul(ps[:, 512:1024], stat,
                            wdT[:, ic * D + 512:(ic + 1) * D],
                            start=(ic == 0), stop=(ic == IC - 1))
        return last

    down_evac = [None] * NT
    t_ostores = [None] * NT
    ln2_prev = None
    for t in range(NT):
        ps = att_ps[t % 4]
        deps = tuple(t_hqT[t // 8]) + tuple(t_wdT)
        if t >= 4:
            deps = deps + (down_evac[t - 4],)
        else:
            deps = deps + (psum_guard.get(t % 4),)
        if t < 4:
            hh = t // 8
            dep1 = tuple(t_hqT[hh][0:11]) + tuple(t_wdT[0:11])
            dep1 = dep1 + (psum_guard.get(t % 4),)
            sched.op("pe", lambda e, t=t, ps=ps: mk_dmm(e, t, ps, 0, 11),
                     deps=dep1)
            t_mm = sched.op("pe", lambda e, t=t, ps=ps: mk_dmm(e, t, ps, 11, IC),
                            deps=tuple(t_hqT[hh][11:]) + tuple(t_wdT[11:]))
        else:
            t_mm = sched.op("pe", lambda e, t=t, ps=ps: mk_dmm(e, t, ps, 0, IC),
                            deps=deps)
        yslot = y1r[:, (t % 3) * D:(t % 3 + 1) * D]
        ydeps = (t_hq_store[NT - 1], wd_store[14], wd_store[15])
        if t >= 3:
            ydeps = ydeps + (down_evac[t - 3],)
        t_yl = sched.dma("sp", yslot, y1_dram[t * 128:(t + 1) * 128, :],
                         deps=ydeps)
        oslot = outb[:, (t % 3) * D:(t % 3 + 1) * D]
        odeps = (t_mm, t_yl, t_hq_store[NT - 1], wd_store[15])
        if t >= 3:
            odeps = odeps + (t_ostores[t - 3],)
        t_ev = sched.op("dve",
                        lambda e, ps=ps, yslot=yslot, oslot=oslot: e.tensor_tensor(
                            oslot, ps[:, :], yslot, ALU.add), deps=odeps)
        down_evac[t] = t_ev
        t_bn0 = sched.op("dve", lambda e, oslot=oslot: e.bn_stats(
            bnbuf[:, 0:6], oslot[:, 0:512]), deps=(t_ev, ln2_prev))
        t_bn = sched.op("dve", lambda e, oslot=oslot: e.bn_stats(
            bnbuf[:, 6:12], oslot[:, 512:1024]), deps=(t_bn0,))
        t_ag = sched.op("dve", lambda e: e.bn_aggr(mvst[:, 2:4], bnbuf[:]),
                        deps=(t_bn,))
        t_sq = sched.op("act", lambda e: e.activation(
            sqc[:], mvst[:, 3:4], AF.Sqrt, bias=epsc[:, 0:1]), deps=(t_ag,))
        t_rsd = sched.op("dve", lambda e: e.reciprocal(rstdc[:], sqc[:]),
                         deps=(t_sq,))
        t_nm0 = sched.op("dve", lambda e: e.tensor_scalar(
            nmr[:], mvst[:, 2:3], rstdc[:, 0:1], -1.0, ALU.mult, ALU.mult),
            deps=(t_rsd,))
        t_nm = sched.op("act", lambda e, oslot=oslot: e.activation(
            oslot, oslot, AF.Identity, bias=nmr[:, 0:1], scale=rstdc[:, 0:1]),
            deps=(t_nm0, t_sq))
        t_ostores[t] = sched.dma("sp", out_d[t * 128:(t + 1) * 128, :],
                                 oslot, deps=(t_nm,))
        ln2_prev = t_nm

    finals = sched.all_dma_barrier("sp")
    sched.finalize()
    with es, nc.Block() as block:
        @block.sync
        def _(sync):
            sched.replay("sp", sync)
            for (sname, v) in finals:
                sync.wait_ge(sched.sems[sname], v)

        @block.vector
        def _(v):
            sched.replay("dve", v)

        @block.scalar
        def _(s):
            sched.replay("act", s)

        @block.gpsimd
        def _(g):
            sched.replay("pool", g)

        @block.tensor
        def _(t):
            sched.replay("pe", t)
    return nc


_NC = None


def kernel(L0_attn_input_data, L0_attn_W1, L0_attn_W2, L0_attn_bias,
           L0_attn_embeddings_table, L0_ffn_w_gate, L0_ffn_w_up,
           L0_ffn_w_down):
    global _NC
    x = np.ascontiguousarray(np.asarray(L0_attn_input_data, dtype=np.float32))
    W1 = np.ascontiguousarray(np.asarray(L0_attn_W1, dtype=np.float32))
    W2 = np.ascontiguousarray(np.asarray(L0_attn_W2, dtype=np.float32))
    bias = np.ascontiguousarray(
        np.asarray(L0_attn_bias, dtype=np.float32)).reshape(1, D)
    et = np.ascontiguousarray(
        np.asarray(L0_attn_embeddings_table, dtype=np.float32))
    wg = np.ascontiguousarray(np.asarray(L0_ffn_w_gate, dtype=np.float32))
    wu = np.ascontiguousarray(np.asarray(L0_ffn_w_up, dtype=np.float32))
    wd = np.ascontiguousarray(np.asarray(L0_ffn_w_down, dtype=np.float32))

    if _NC is None:
        _NC = build_fused()
    ident = np.eye(21, dtype=np.float32)
    eye128 = np.eye(128, dtype=np.float32)
    ins = [{"x": x[c], "w1": W1, "w2": W2, "bias": bias, "et": et,
            "ident": ident, "eye128": eye128,
            "wg": wg, "wu": wu, "wd": wd} for c in range(NCORES)]
    res = run_bass_kernel_spmd(_NC, ins, core_ids=list(range(NCORES)))
    out = np.stack([res.results[c]["out"] for c in range(NCORES)], axis=0)
    return out.astype(np.float32)


# version 18
# speedup vs baseline: 1.0007x; 1.0007x over previous
import numpy as np
import concourse.bass as bass
import concourse.mybir as mybir
from concourse.bass_utils import run_bass_kernel_spmd
from contextlib import ExitStack

F32 = mybir.dt.float32
BF16 = mybir.dt.bfloat16
FP16 = mybir.dt.float16
ALU = mybir.AluOpType
AF = mybir.ActivationFunctionType
AX = mybir.AxisListType

NCORES = 8
N, D, I = 2048, 1024, 2816
NT = N // 128
KC = D // 128
IC = I // 128
RND_HC = 6291456.0
VELT_C = 4194305.0
PE_TAPS = list(range(1, 21))
PAD = 32
QW = 1024


class Sched:

    ENGS = ("pe", "dve", "act", "pool", "sp")
    NPOOL = {"act": 12, "pool": 8, "sp": 20}

    def __init__(self, nc):
        self.nc = nc
        self.items = {e: [] for e in self.ENGS}
        self.count = {}
        self.sems = {}
        self.rot = {e: 0 for e in self.NPOOL}

    def make_sems(self, es):
        for e in self.ENGS:
            self.sems[e] = es.enter_context(self.nc.semaphore(f"sem_{e}"))
            self.count[e] = 0
        for e, np_ in self.NPOOL.items():
            for i in range(np_):
                q = f"{e}q{i}"
                self.sems[q] = es.enter_context(self.nc.semaphore(f"sem_{q}"))
                self.count[q] = 0

    def op(self, eng, fn, deps=()):
        self.count[eng] += 1
        tok = (eng, self.count[eng])
        self.items[eng].append(
            (fn, tuple(d for d in deps if d is not None), tok, 1, eng))
        return tok

    def dma(self, eng, out, in_, deps=(), **kw):
        assert eng in self.NPOOL
        q = f"{eng}q{self.rot[eng]}"
        self.rot[eng] = (self.rot[eng] + 1) % self.NPOOL[eng]
        prev = self.count[q]
        deps = tuple(d for d in deps if d is not None)
        if prev > 0:
            deps = deps + ((q, prev),)
        self.count[q] += 16
        tok = (q, self.count[q])
        self.items[eng].append(
            (lambda e: e.dma_start(out=out, in_=in_, **kw), deps, tok, 16, q))
        return tok

    def finalize(self):
        self.prior = {}
        for eng in self.ENGS:
            run = {}
            for (fn, deps, tok, inc, sem) in self.items[eng]:
                self.prior[tok] = dict(run)
                run[sem] = tok[1]

    def replay(self, eng, e):
        waited = {}
        for (fn, deps, tok, inc, sem) in self.items[eng]:
            want = {}
            for (s, v) in deps:
                if s == eng:
                    v = min(v, self.prior[tok].get(s, 0))
                if v > want.get(s, 0):
                    want[s] = v
            for s, v in want.items():
                if v > waited.get(s, 0):
                    e.wait_ge(self.sems[s], v)
                    waited[s] = v
            inst = fn(e)
            if inst is not None:
                inst.then_inc(self.sems[sem], inc)

    def all_dma_barrier(self, eng):
        return tuple((f"{eng}q{i}", self.count[f"{eng}q{i}"])
                     for i in range(self.NPOOL[eng])
                     if self.count[f"{eng}q{i}"] > 0)


class Arena:
    def __init__(self, nc, total_bytes):
        self.nc = nc
        self.total = total_bytes
        self.handle = None
        self.named = []

    def reserve(self, es):
        self.handle = es.enter_context(
            self.nc.sbuf_tensor("arena", [128, self.total], mybir.dt.uint8))

    def at(self, name, offset, shape, dtype):
        per_part = int(np.prod(shape[1:])) * mybir.dt.size(dtype)
        assert offset % 32 == 0, (name, offset)
        assert offset + per_part <= self.total, (name, offset, per_part)
        h = self.nc.virtual_sbuf_tensor(name, shape, dtype)
        self.named.append((h, offset))
        return h

    def finalize(self):
        base = self.nc.lookup_mloc(self.handle).addr
        for h, off in self.named:
            self.nc._tensor(h.name, h.shape, h.dtype, type="SB")
            mloc = self.nc.lookup_mloc(h)
            mloc.allocated = not self.nc.target_bir_lowering
            mloc.addr = base + off


MAP_P = {"u": "pool", "w": "pool", "hi": "pool", "G1": "pool", "q": "pool"}
MAP_D = {"u": "pool", "w": "dve", "hi": "dve", "G1": "pool", "q": "dve"}


def qv_stages(sched, bufs, src_ap, q_out_ap, width, src_tok,
              extra_q_deps, prev1, prev2, emap, T):
    nb = width // 16
    amax, s_t, rs_t = bufs["amax"], bufs["s"], bufs["rs"]
    u, cu, w = bufs["u"], bufs["cu"], bufs["w"]
    hi, Ra, Rb, lcp = bufs["hi"], bufs["Ra"], bufs["Rb"], bufs["lcp"]
    p1 = prev1 or {}
    p2 = prev2 or {}

    def g(d, *keys):
        return tuple(d[k] for k in keys if d.get(k) is not None)

    src_deps = tuple(src_tok) if isinstance(src_tok, (tuple, list)) and \
        src_tok and isinstance(src_tok[0], tuple) else (src_tok,)
    src3 = src_ap.rearrange("p (b k) -> p b k", k=16)
    rs_b = rs_t[:, :nb].unsqueeze(2).broadcast_to([128, nb, 16])
    s_b = s_t[:, :nb].unsqueeze(2).broadcast_to([128, nb, 16])
    u3 = u[:, :width].rearrange("p (b k) -> p b k", k=16)
    uf, cuf, wf = u[:, :width], cu[:, :width], w[:, :width]
    hif, raf, rbf, lcpf = (hi[:, :width], Ra[:, :width], Rb[:, :width],
                           lcp[:, :width])
    ctok = tuple(bufs.get("ctok") or ())
    cC = bufs["cC"][:, 0:1]
    cH = bufs["cH"][:, 0:1]
    cn1 = bufs["cn1"][:, 0:1]
    q3 = q_out_ap.rearrange("p (b k) -> p b k", k=16)
    hi3 = hif.rearrange("p (b k) -> p b k", k=16)

    def st_am():
        T["am"] = sched.op("dve", lambda e: e.tensor_reduce(
            amax[:, :nb], src3, AX.X, ALU.max, apply_absolute_value=True),
            deps=src_deps + g(p2, "s"))

    def st_s():
        T["s"] = sched.op("dve", lambda e: e.tensor_scalar(
            s_t[:, :nb], amax[:, :nb], 1.0 / 6.0, 1e-12, ALU.mult, ALU.max),
            deps=(T["am"],) + g(p2, "q", "rs"))

    def st_rs():
        T["rs"] = sched.op("dve", lambda e: e.reciprocal(
            rs_t[:, :nb], s_t[:, :nb]), deps=(T["s"],) + g(p2, "u"))

    def st_u():
        T["u"] = sched.op(emap["u"], lambda e: e.tensor_tensor(
            u3, src3, rs_b, ALU.mult), deps=(T["rs"],) + g(p1, "b5p"))

    def st_cu():
        T["cu"] = sched.op("act", lambda e: e.activation(
            cuf, uf, AF.Identity, scale=cC),
            deps=(T["u"],) + g(p1, "lcp") + ctok)

    def st_w():
        T["w"] = sched.op(emap["w"], lambda e: e.tensor_tensor(
            wf, cuf, uf, ALU.subtract), deps=(T["cu"],) + g(p1, "G"))

    def st_hi():
        T["hi"] = sched.op(emap["hi"], lambda e: e.tensor_tensor(
            hif, cuf, wf, ALU.subtract), deps=(T["w"],) + g(p1, "q"))

    def st_b5p():
        T["b5p"] = sched.op("act", lambda e: e.activation(
            cuf, uf, AF.Identity, bias=cH), deps=(T["hi"],))

    def st_Ra():
        T["Ra"] = sched.op("act", lambda e: e.activation(
            raf, hif, AF.Relu, bias=cn1), deps=(T["hi"],))

    def st_Rb():
        T["Rb"] = sched.op("act", lambda e: e.activation(
            rbf, hif, AF.Relu, bias=cn1, scale=cn1), deps=(T["Ra"],))

    def st_lcp():
        T["lcp"] = sched.op("dve", lambda e: e.tensor_scalar(
            lcpf, cuf, RND_HC, -1.0, ALU.subtract, ALU.max),
            deps=(T["b5p"],))

    def st_G1():
        T["G1"] = sched.op(emap["G1"], lambda e: e.tensor_tensor(
            raf, raf, rbf, ALU.subtract), deps=(T["Rb"],))

    def st_G():
        T["G"] = sched.op("dve", lambda e: e.scalar_tensor_tensor(
            hif, lcpf, 1.0, raf, ALU.min, ALU.add),
            deps=(T["lcp"], T["G1"]))

    def st_q():
        T["q"] = sched.op(emap["q"], lambda e: e.tensor_tensor(
            q3, hi3, s_b, ALU.mult), deps=(T["G"],) + tuple(extra_q_deps))

    return [st_am, st_s, st_rs, st_u, st_cu, st_w, st_hi, st_b5p,
            st_Ra, st_Rb, st_lcp, st_G1, st_G, st_q]


def emit_quant_v(sched, bufs, src_ap, q_out_ap, width, src_tok,
                 extra_q_deps=(), prev1=None, prev2=None, q_on_dve=False,
                 emap=None):
    T = {}
    emap = emap or (dict(MAP_P, q="dve") if q_on_dve else MAP_P)
    for st in qv_stages(sched, bufs, src_ap, q_out_ap, width, src_tok,
                        extra_q_deps, prev1, prev2, emap, T):
        st()
    return T


def emit_quant_pair(sched, specs):
    Ts = [dict() for _ in specs]
    plans = [qv_stages(sched, b, s, q, wd, tok, ex, p1, p1, em, Ts[i])
             for i, (b, s, q, wd, tok, ex, p1, em) in enumerate(specs)]
    for si in range(len(plans[0])):
        for p in plans:
            p[si]()
    return Ts


def build_fused():
    nc = bass.Bass("TRN2", target_bir_lowering=False, debug=False)
    x_d = nc.dram_tensor("x", [N, D], F32, kind="ExternalInput").ap()
    w1_d = nc.dram_tensor("w1", [D, D], F32, kind="ExternalInput").ap()
    w2_d = nc.dram_tensor("w2", [D, D], F32, kind="ExternalInput").ap()
    bias_d = nc.dram_tensor("bias", [1, D], F32, kind="ExternalInput").ap()
    et_d = nc.dram_tensor("et", [21, D], F32, kind="ExternalInput").ap()
    id_d = nc.dram_tensor("ident", [21, 21], F32, kind="ExternalInput").ap()
    eye_d = nc.dram_tensor("eye128", [128, 128], F32, kind="ExternalInput").ap()
    wg_d = nc.dram_tensor("wg", [I, D], F32, kind="ExternalInput").ap()
    wu_d = nc.dram_tensor("wu", [I, D], F32, kind="ExternalInput").ap()
    wd_d = nc.dram_tensor("wd", [D, I], F32, kind="ExternalInput").ap()
    out_d = nc.dram_tensor("out", [N, D], F32, kind="ExternalOutput").ap()

    xb_dram = nc.dram_tensor("xb_dram", [N, D], FP16, kind="Internal").ap()
    y1_dram = nc.dram_tensor("y1_dram", [N, D], F32, kind="Internal").ap()
    y1q_dram = nc.dram_tensor("y1q_dram", [N, D], BF16, kind="Internal").ap()
    hq_dram = nc.dram_tensor("hq_dram", [N, I], BF16, kind="Internal").ap()
    wgq_d = nc.dram_tensor("wgq", [I, D], BF16, kind="Internal").ap()
    wuq_d = nc.dram_tensor("wuq", [I, D], BF16, kind="Internal").ap()
    wdq_d = nc.dram_tensor("wdq", [D, I], BF16, kind="Internal").ap()

    es = ExitStack()
    sched = Sched(nc)
    sched.make_sems(es)

    AR = 207 * 1024
    ar = Arena(nc, AR)
    ar.reserve(es)

    y1qT = ar.at("y1qT", 0, [128, KC * N], BF16)
    convA = ar.at("convA", 0, [128, N], FP16)
    convB = ar.at("convB", 4096, [128, N], FP16)
    convC = ar.at("convC", 8192, [128, N], FP16)
    convD = ar.at("convD", 12288, [128, N], FP16)
    convE = ar.at("convE", 16384, [128, N], FP16)
    et_rows = ar.at("et_rows", 20480, [21, D], F32)
    ident21 = ar.at("ident21", 24576, [21, 21], F32)

    GEN = 32768
    x_res = ar.at("x_res", GEN, [128, NT * D], F32)
    xT = ar.at("xT", GEN + 65536, [128, KC * (N + PAD)], FP16)
    hT = ar.at("hT", GEN + 98816, [128, KC * N], FP16)
    W1b = ar.at("W1b", GEN + 131584, [128, KC * D], FP16)
    W2b = ar.at("W2b", GEN + 148096, [128, KC * D], FP16)

    def mk_qset(tag, W, u_o, cu_o, w_o, hi_o, ra_o, rb_o, am_o, s_o, rs_o,
                src_o=None, q_o=None):
        d = {"u": ar.at(f"{tag}_u", u_o, [128, W], F32),
             "cu": ar.at(f"{tag}_cu", cu_o, [128, W], F32),
             "w": ar.at(f"{tag}_w", w_o, [128, W], F32),
             "lcp": ar.at(f"{tag}_lcp", w_o, [128, W], BF16),
             "hi": ar.at(f"{tag}_hi", hi_o, [128, W], BF16),
             "Ra": ar.at(f"{tag}_Ra", ra_o, [128, W], BF16),
             "Rb": ar.at(f"{tag}_Rb", rb_o, [128, W], BF16),
             "amax": ar.at(f"{tag}_am", am_o, [128, W // 16], F32),
             "s": ar.at(f"{tag}_s", s_o, [128, W // 16], F32),
             "rs": ar.at(f"{tag}_rs", rs_o, [128, W // 16], F32)}
        if src_o is not None:
            d["src"] = ar.at(f"{tag}_src", src_o, [128, W], F32)
        if q_o is not None:
            d["qout"] = ar.at(f"{tag}_q", q_o, [128, W], BF16)
        return d

    vw2 = [mk_qset("vw0", 1024, 49152, 53248, 57344, 61440, 63488, 65536,
                   69632, 69888, 70144, src_o=45056, q_o=67584),
           mk_qset("vw1", 1024, 74496, 78592, 82688, 86784, 88832, 90880,
                   94976, 95232, 95488, src_o=70400, q_o=92928)]

    vy1 = {
        "u": ar.at("vy_u", 197248, [128, QW], F32),
        "cu": ar.at("vy_cu", 201344, [128, QW], F32),
        "w": ar.at("vy_w", 16384, [128, QW], F32),
        "lcp": ar.at("vy_lcp", 16384, [128, QW], BF16),
        "hi": ar.at("vy_hi", 2048, [128, QW], BF16),
        "Ra": ar.at("vy_Ra", 6144, [128, QW], BF16),
        "Rb": ar.at("vy_Rb", 10240, [128, QW], BF16),
        "amax": ar.at("vy_amax", 205440, [128, QW // 16], F32),
        "s": ar.at("vy_s", 205728, [128, QW // 16], F32),
        "rs": ar.at("vy_rs", 206016, [128, QW // 16], F32),
    }
    y1q_st = [ar.at("y1qs0", 26624, [128, D], BF16),
              ar.at("y1qs1", 30720, [128, D], BF16)]
    diag16 = ar.at("diag16", 197248, [128, len(PE_TAPS) * 128], FP16)

    wgT = ar.at("wgT", 32768, [128, KC * I], BF16)
    wuT = ar.at("wuT", 77824, [128, KC * I], BF16)
    vh2 = [mk_qset("vh0", 1408, 122880, 134144, 151040, 145408, 175776,
                   181408, 162304, 163040, 163776),
           mk_qset("vh1", 1408, 128512, 139776, 156672, 148224, 178592,
                   184224, 162656, 163392, 164128)]
    h_t = ar.at("h_t", 164512, [128, I], F32)
    hq_t = ar.at("hq_t", 187040, [128, I], BF16)
    wd_src = ar.at("wd_src", 192672, [128, 1408], F32)
    sg_b = [None] * 4
    up_b = [None] * 4
    cur2 = 199456
    for i in range(4):
        sg_b[i] = ar.at(f"sg{i}", cur2, [128, 512], BF16); cur2 += 1024
        up_b[i] = ar.at(f"up{i}", cur2, [128, 512], BF16); cur2 += 1024
    assert cur2 <= 208160, cur2

    hqT = ar.at("hqT", 32768, [128, IC * N], BF16)
    wdT = ar.at("wdT", 122880, [128, IC * D], BF16)
    y1r = ar.at("y1r", 167936, [128, 3 * D], F32)
    outb = ar.at("outb", 180224, [128, 3 * D], F32)

    sm = 208160
    et_fm = ar.at("et_fm", sm, [128, KC * 21], F32); sm += KC * 21 * 4
    sm = (sm + 31) // 32 * 32
    biasb = ar.at("biasb", sm, [1, D], BF16); sm += D * 2
    onesb = ar.at("onesb", sm, [1, 128], BF16); sm += 256
    epsc = ar.at("epsc", sm, [128, 1], F32); sm += 32
    mvst = ar.at("mvst", sm, [128, 8], F32); sm += 32
    bnbuf = ar.at("bnbuf", sm, [128, 12], F32); sm += 64
    rstdc = ar.at("rstdc", sm, [128, 1], F32); sm += 32
    sqc = ar.at("sqc", sm, [128, 1], F32); sm += 32
    nmr = ar.at("nmr", sm, [128, 1], F32); sm += 32
    eye16 = ar.at("eye16", sm, [128, 128], FP16); sm += 256
    cvelt = ar.at("cvelt", sm, [128, 1], F32); sm += 32
    crndhc = ar.at("crndhc", sm, [128, 1], F32); sm += 32
    cneg1 = ar.at("cneg1", sm, [128, 1], F32); sm += 32
    assert sm <= AR, sm

    att_ps = [es.enter_context(nc.psum_tensor(f"aps{i}", [128, 1024], F32))
              for i in range(4)]

    ar.finalize()

    t_ep = sched.op("dve", lambda e: e.memset(epsc[:], 1e-5))
    t_ones = sched.op("dve", lambda e: e.memset(onesb[:], 1.0))
    ctok = (sched.op("dve", lambda e: e.memset(cvelt[:], VELT_C)),
            sched.op("dve", lambda e: e.memset(crndhc[:], RND_HC)),
            sched.op("dve", lambda e: e.memset(cneg1[:], -1.0)))
    consts = {"cC": cvelt, "cH": crndhc, "cn1": cneg1, "ctok": ctok}
    for b in vw2 + vh2 + [vy1]:
        b.update(consts)
    t_pad = [sched.op("dve", lambda e, k=k: e.memset(
        xT[:, k * (N + PAD): k * (N + PAD) + PAD], 0.0)) for k in range(KC)]

    t_xres = [None] * NT
    for t in range(3):
        t_xres[t] = sched.dma("sp", x_res[:, t * D:(t + 1) * D],
                              x_d[t * 128:(t + 1) * 128, :])

    wjobs = []
    for src_d, dst_d in ((wg_d, wgq_d), (wu_d, wuq_d)):
        for j in range(IC):
            wjobs.append((src_d[j * 128:(j + 1) * 128, :],
                          dst_d[j * 128:(j + 1) * 128, :]))
    NWJ = len(wjobs)
    wload = [None] * NWJ
    wstore = [None] * NWJ
    wq_prev = [None, None]
    wload[0] = sched.dma("sp", vw2[0]["src"][:, :], wjobs[0][0])
    wload[1] = sched.dma("sp", vw2[1]["src"][:, :], wjobs[1][0])

    def emit_wpair(j):
        specs = []
        for o, em in ((0, MAP_P), (1, MAP_D)):
            p = (j + o) % 2
            S = vw2[p]
            ex = (wstore[j + o - 2],) if j + o >= 2 else ()
            specs.append((S, S["src"][:, :], S["qout"][:, :], 1024,
                          (wload[j + o],), ex, wq_prev[p], em))
        Ts = emit_quant_pair(sched, specs)
        for o in (0, 1):
            p = (j + o) % 2
            wq_prev[p] = Ts[o]
            wstore[j + o] = sched.dma("sp", wjobs[j + o][1],
                                      vw2[p]["qout"][:, :],
                                      deps=(Ts[o]["q"],))
            if j + o + 2 < NWJ:
                wload[j + o + 2] = sched.dma("sp", vw2[p]["src"][:, :],
                                             wjobs[j + o + 2][0],
                                             deps=(Ts[o]["u"],))

    t_xb = sched.dma("pool", xb_dram[:], x_d[:])
    t_xT = [sched.dma("sp", xT[:, k * (N + PAD) + PAD:(k + 1) * (N + PAD)],
                      xb_dram[:, k * 128:(k + 1) * 128],
                      deps=(t_xb, t_pad[k]), transpose=True) for k in range(KC)]
    t_w1 = [sched.dma("pool", W1b[:, k * D:(k + 1) * D],
                      w1_d[k * 128:(k + 1) * 128, :]) for k in range(KC)]
    t_w2 = [sched.dma("pool", W2b[:, k * D:(k + 1) * D],
                      w2_d[k * 128:(k + 1) * 128, :]) for k in range(KC)]
    t_bias = sched.dma("pool", biasb[:], bias_d[:])
    t_eye = sched.dma("pool", eye16[:], eye_d[:])
    t_etl = sched.dma("sp", et_rows[:], et_d[:])
    t_id = [sched.dma("sp", ident21[:], id_d[:])]
    t_etfm = []
    for k in range(KC):
        t_tr = sched.op("pe", lambda e, k=k: e.transpose(
            att_ps[0][:, 0:21], et_rows[0:21, k * 128:(k + 1) * 128], ident21[:]),
            deps=tuple(t_id) + (t_etl, t_etfm[-1] if t_etfm else None))
        t_cp = sched.op("dve", lambda e, k=k: e.tensor_copy(
            et_fm[:, k * 21:(k + 1) * 21], att_ps[0][:, 0:21]), deps=(t_tr,))
        t_etfm.append(t_cp)
    t_et_done = t_etfm[-1]

    def xsh(k, j):
        base = k * (N + PAD) + PAD - j
        return xT[:, base: base + N]

    def etc(k, j):
        return et_fm[:, k * 21 + j: k * 21 + j + 1]

    conv_done = []
    conv_ps = [att_ps[3][:, 0:512], att_ps[3][:, 512:1024]]
    cp_guard = [None, None]
    cpar = [0]
    wj_next = [0]
    for k in range(KC):
        npe = len(PE_TAPS)
        eye_b = eye16[:].unsqueeze(1).broadcast_to([128, npe, 128])
        etv = et_fm[:, k * 21 + PE_TAPS[0]: k * 21 + PE_TAPS[0] + npe]
        et_b = etv.unsqueeze(2).broadcast_to([128, npe, 128])
        t_dg = sched.op("dve", lambda e, k=k, eye_b=eye_b, et_b=et_b: e.tensor_tensor(
            diag16[:].rearrange("p (j m) -> p j m", m=128), eye_b, et_b,
            ALU.mult), deps=(t_et_done, t_eye) +
            (tuple(conv_done[-1]) if conv_done else ()))
        pe_evacs = []
        for nt in range(4):
            par = cpar[0] % 2
            cpar[0] += 1

            def mk_conv(e, k=k, nt=nt, par=par):
                last = None
                for ji, j in enumerate(PE_TAPS):
                    e.ldweights(diag16[:, ji * 128:(ji + 1) * 128])
                    last = e.matmul(
                        conv_ps[par], diag16[:, ji * 128:(ji + 1) * 128],
                        xT[:, k * (N + PAD) + PAD - j + nt * 512:
                           k * (N + PAD) + PAD - j + nt * 512 + 512],
                        start=(ji == 0), stop=(ji == len(PE_TAPS) - 1))
                return last
            cdeps = (t_dg, t_xT[k])
            if cp_guard[par] is not None:
                cdeps = cdeps + (cp_guard[par],)
            t_cmm = sched.op("pe", mk_conv, deps=cdeps)
            t_evc = sched.op("act", lambda e, k=k, nt=nt, par=par: e.activation(
                hT[:, k * N + nt * 512: k * N + (nt + 1) * 512],
                conv_ps[par], AF.Identity), deps=(t_cmm,))
            cp_guard[par] = t_evc
            pe_evacs.append(t_evc)
        conv_done.append(tuple(pe_evacs))
        for _ in range(3):
            if wj_next[0] < NWJ:
                emit_wpair(wj_next[0])
                wj_next[0] += 2
    conv_done = [t for evs in conv_done for t in evs]
    assert wj_next[0] == NWJ
    for t in range(3, NT):
        t_xres[t] = sched.dma("sp", x_res[:, t * D:(t + 1) * D],
                              x_d[t * 128:(t + 1) * 128, :],
                              deps=(wstore[NWJ - 1], wstore[NWJ - 2]))

    def mk_mm(e, t, ps):
        last = None
        for k in range(KC):
            stat = xT[:, k * (N + PAD) + PAD + t * 128:
                      k * (N + PAD) + PAD + (t + 1) * 128]
            e.ldweights(stat)
            e.matmul(ps[:, 0:512], stat, W1b[:, k * D: k * D + 512],
                     start=(k == 0), stop=False)
            last = e.matmul(ps[:, 512:1024], stat,
                            W1b[:, k * D + 512:(k + 1) * D],
                            start=(k == 0), stop=False)
        return last

    def mk_mm2(e, t, ps):
        for k in range(KC):
            stat = hT[:, k * N + t * 128: k * N + (t + 1) * 128]
            e.ldweights(stat)
            e.matmul(ps[:, 0:512], stat, W2b[:, k * D: k * D + 512],
                     start=False, stop=False)
            e.matmul(ps[:, 512:1024], stat, W2b[:, k * D + 512:(k + 1) * D],
                     start=False, stop=False)
        e.ldweights(onesb[0:1, :])
        e.matmul(ps[:, 0:512], onesb[0:1, :], biasb[0:1, 0:512],
                 start=False, stop=True)
        return e.matmul(ps[:, 512:1024], onesb[0:1, :], biasb[0:1, 512:1024],
                        start=False, stop=True)

    attn_evac = [None] * NT
    tok_x = [None] * NT
    t_y1d = [None] * NT
    ln_prev = None
    y1q_tok = [None] * NT
    y1q_dump = [None] * NT
    qprev = {}
    prev_dump = {0: None, 1: None}

    for t in range(3):
        tok_x[t] = sched.op("pe", lambda e, t=t: mk_mm(e, t, att_ps[t % 3]),
                            deps=tuple(t_xT) + tuple(t_w1) + (t_et_done,))
    for t in range(NT):
        ps = att_ps[t % 3]
        tok_h = sched.op("pe", lambda e, t=t, ps=ps: mk_mm2(e, t, ps),
                         deps=tuple(conv_done) + tuple(t_w2) +
                         (tok_x[t], t_ones, t_bias))
        t_ev = sched.op("dve", lambda e, t=t, ps=ps: e.tensor_tensor(
            x_res[:, t * D:(t + 1) * D], ps[:, :], x_res[:, t * D:(t + 1) * D],
            ALU.add), deps=(tok_h, t_xres[t]))
        attn_evac[t] = t_ev
        if t + 3 < NT:
            tok_x[t + 3] = sched.op(
                "pe", lambda e, t=t: mk_mm(e, t + 3, att_ps[(t + 3) % 3]),
                deps=(t_ev,))
        t_bn0 = sched.op("dve", lambda e, t=t: e.bn_stats(
            bnbuf[:, 0:6], x_res[:, t * D: t * D + 512]), deps=(t_ev, ln_prev))
        t_bn = sched.op("dve", lambda e, t=t: e.bn_stats(
            bnbuf[:, 6:12], x_res[:, t * D + 512:(t + 1) * D]), deps=(t_bn0,))
        t_ag = sched.op("dve", lambda e: e.bn_aggr(mvst[:, 0:2], bnbuf[:]),
                        deps=(t_bn,))
        t_sq = sched.op("act", lambda e: e.activation(
            sqc[:], mvst[:, 1:2], AF.Sqrt, bias=epsc[:, 0:1]), deps=(t_ag, t_ep))
        t_rsd = sched.op("dve", lambda e: e.reciprocal(rstdc[:], sqc[:]),
                         deps=(t_sq,))
        t_nm0 = sched.op("dve", lambda e: e.tensor_scalar(
            nmr[:], mvst[:, 0:1], rstdc[:, 0:1], -1.0, ALU.mult, ALU.mult),
            deps=(t_rsd,))
        t_nm = sched.op("act", lambda e, t=t: e.activation(
            x_res[:, t * D:(t + 1) * D], x_res[:, t * D:(t + 1) * D],
            AF.Identity, bias=nmr[:, 0:1], scale=rstdc[:, 0:1]),
            deps=(t_nm0, t_sq))
        ln_prev = t_nm
        t_y1d[t] = sched.dma("sp", y1_dram[t * 128:(t + 1) * 128, :],
                             x_res[:, t * D:(t + 1) * D], deps=(t_nm,))
        stg = y1q_st[t % 2]
        qdeps = ((prev_dump[t % 2],) if prev_dump[t % 2] else ())
        Tq = emit_quant_v(sched, vy1, x_res[:, t * D:(t + 1) * D],
                          stg[:], QW, (t_nm,), extra_q_deps=qdeps,
                          prev1=qprev.get(t - 1), prev2=qprev.get(t - 1))
        qprev[t] = Tq
        y1q_tok[t] = Tq["q"]
        y1q_dump[t] = sched.dma("sp", y1q_dram[t * 128:(t + 1) * 128, :],
                                stg[:], deps=(Tq["q"],))
        prev_dump[t % 2] = y1q_dump[t]

    y1qT_half = [None, None]
    for half in range(2):
        lst = []
        for k in range(KC):
            dd = tuple(y1q_dump[half * 8:(half + 1) * 8])
            if (half == 0 and k == 4) or (half == 1 and k in (0, 1, 2, 4)):
                dd = dd + (y1q_tok[NT - 1],)
            lst.append(sched.dma(
                "sp", y1qT[:, k * N + half * 1024: k * N + half * 1024 + 1024],
                y1q_dram[half * 1024:(half + 1) * 1024, k * 128:(k + 1) * 128],
                deps=dd, transpose=True))
        y1qT_half[half] = lst

    def xdead(byte0, byte1):
        t0 = max(0, (byte0 - 32768) // 4096)
        t1 = min(NT - 1, (byte1 - 1 - 32768) // 4096)
        dd = []
        for tt in range(t0, t1 + 1):
            dd += [y1q_tok[tt], t_y1d[tt]]
        return tuple(dd)
    wsts = tuple(wstore)
    t_wgT = [sched.dma("sp", wgT[:, k * I:(k + 1) * I],
                       wgq_d[:, k * 128:(k + 1) * 128], transpose=True,
                       deps=xdead(32768 + k * 5632, 32768 + (k + 1) * 5632) +
                       wsts)
             for k in range(KC)]
    t_wuT = [sched.dma("sp", wuT[:, k * I:(k + 1) * I],
                       wuq_d[:, k * 128:(k + 1) * 128], transpose=True,
                       deps=xdead(77824 + k * 5632, 77824 + (k + 1) * 5632) +
                       ((attn_evac[NT - 1],) if k >= 4 else ()) + wsts)
             for k in range(KC)]

    IBANKS = [(ib * 512, min(512, I - ib * 512)) for ib in range((I + 511) // 512)]
    t_hq_store = [None] * NT
    wd_store = [None] * 16
    psum_guard = {}
    h_guard = [()]
    range_prev = [None, None]
    chain2 = [None, None]
    tog = [0]
    h_parts_of = [None] * NT
    pslot = [0]

    def emit_ffn_tile(t):
        h_parts = []
        for bi, (i0, w) in enumerate(IBANKS):
            ps = att_ps[pslot[0] % 4]
            sg = sg_b[pslot[0] % 4]
            uv = up_b[pslot[0] % 4]
            deps = tuple(t_wgT) + tuple(t_wuT) + tuple(y1qT_half[t // 8])
            if psum_guard.get(pslot[0] % 4) is not None:
                deps = deps + (psum_guard[pslot[0] % 4],)
            pslot[0] += 1

            def mk_gmm(e, t=t, i0=i0, w=w, ps=ps):
                last = None
                for k in range(KC):
                    stat = y1qT[:, k * N + t * 128: k * N + (t + 1) * 128]
                    e.ldweights(stat)
                    e.matmul(ps[:, 0:w], stat, wgT[:, k * I + i0: k * I + i0 + w],
                             start=(k == 0), stop=(k == KC - 1))
                    last = e.matmul(ps[:, 512:512 + w], stat,
                                    wuT[:, k * I + i0: k * I + i0 + w],
                                    start=(k == 0), stop=(k == KC - 1))
                return last
            t_mm = sched.op("pe", mk_gmm, deps=deps)
            t_sg = sched.op("act", lambda e, sg=sg, ps=ps, w=w: e.activation(
                sg[:, 0:w], ps[:, 0:w], AF.Silu), deps=(t_mm,))
            t_uv = sched.op("act", lambda e, uv=uv, ps=ps, w=w: e.activation(
                uv[:, 0:w], ps[:, 512:512 + w], AF.Identity), deps=(t_sg,))
            t_hm = sched.op("dve",
                            lambda e, sg=sg, uv=uv, i0=i0, w=w:
                            e.tensor_tensor(h_t[:, i0:i0 + w], sg[:, 0:w],
                                            uv[:, 0:w], ALU.mult),
                            deps=(t_uv,) + h_guard[0])
            psum_guard[(pslot[0] - 1) % 4] = t_hm
            h_parts.append(t_hm)
        h_parts_of[t] = h_parts

    def emit_hquant(t):
        specs = []
        ps = []
        for half, em in ((0, MAP_P), (1, MAP_D)):
            p = tog[0] % 2
            tog[0] += 1
            ps.append(p)
            S = vh2[p]
            c0 = half * 1408
            ex = (range_prev[half],) if range_prev[half] else ()
            specs.append((S, h_t[:, c0:c0 + 1408], hq_t[:, c0:c0 + 1408],
                          1408, tuple(h_parts_of[t]), ex, chain2[p], em))
        Ts = emit_quant_pair(sched, specs)
        for half in range(2):
            chain2[ps[half]] = Ts[half]
        h_guard[0] = (Ts[0]["u"], Ts[1]["u"])
        t_st = sched.dma("sp", hq_dram[t * 128:(t + 1) * 128, :],
                         hq_t[0:128, :], deps=(Ts[0]["q"], Ts[1]["q"]))
        t_hq_store[t] = t_st
        range_prev[0] = t_st
        range_prev[1] = t_st

    wd_jobs = []
    for i in range(8):
        for c in (0, 1408):
            wd_jobs.append((wd_d[i * 128:(i + 1) * 128, c:c + 1408],
                            wdq_d[i * 128:(i + 1) * 128, c:c + 1408]))
    wd_load = [None] * 16
    wd_T = [None] * 16

    def emit_wdjob(i):
        src_ap, dst_ap = wd_jobs[i]
        ldeps = (y1q_tok[NT - 1],) if i == 0 else (wd_T[i - 1]["u"],)
        wd_load[i] = sched.dma("sp", wd_src[:, :], src_ap, deps=ldeps)
        p = tog[0] % 2
        tog[0] += 1
        S = vh2[p]
        Tq = emit_quant_v(sched, S, wd_src[:, :], hq_t[:, 0:1408], 1408,
                          (wd_load[i],),
                          extra_q_deps=((range_prev[0],)
                                        if range_prev[0] else ()),
                          prev1=chain2[p], prev2=chain2[p],
                          emap=(MAP_D if i % 2 else MAP_P))
        chain2[p] = Tq
        wd_T[i] = Tq
        wd_store[i] = sched.dma("sp", dst_ap, hq_t[:, 0:1408], deps=(Tq["q"],))
        range_prev[0] = wd_store[i]

    wd_next = [0]
    for t in range(NT):
        emit_ffn_tile(t)
        emit_hquant(t)
        if t >= 8:
            for _ in range(2):
                if wd_next[0] < 16:
                    emit_wdjob(wd_next[0])
                    wd_next[0] += 1
    assert wd_next[0] == 16

    wdsts = tuple(wd_store)
    ffa_done = tuple(h_parts_of[NT - 1])
    t_hqT = []
    for half in range(2):
        t_hqT.append([sched.dma(
            "sp", hqT[:, ic * N + half * 1024: ic * N + half * 1024 + 1024],
            hq_dram[half * 1024:(half + 1) * 1024, ic * 128:(ic + 1) * 128],
            transpose=True,
            deps=tuple(tk for tk in t_hq_store[half * 8:(half + 1) * 8] if tk) +
            ffa_done)
            for ic in range(IC)])
    t_wdT = [sched.dma("sp", wdT[:, ic * D:(ic + 1) * D],
                       wdq_d[:, ic * 128:(ic + 1) * 128], transpose=True,
                       deps=tuple(t_hq_store) + wdsts) for ic in range(IC)]

    def mk_dmm(e, t, ps, ic0, ic1):
        last = None
        for ic in range(ic0, ic1):
            stat = hqT[:, ic * N + t * 128: ic * N + (t + 1) * 128]
            e.ldweights(stat)
            e.matmul(ps[:, 0:512], stat, wdT[:, ic * D: ic * D + 512],
                     start=(ic == 0), stop=(ic == IC - 1))
            last = e.matmul(ps[:, 512:1024], stat,
                            wdT[:, ic * D + 512:(ic + 1) * D],
                            start=(ic == 0), stop=(ic == IC - 1))
        return last

    down_evac = [None] * NT
    t_ostores = [None] * NT
    ln2_prev = None
    for t in range(NT):
        ps = att_ps[t % 4]
        deps = tuple(t_hqT[t // 8]) + tuple(t_wdT)
        if t >= 4:
            deps = deps + (down_evac[t - 4],)
        else:
            deps = deps + (psum_guard.get(t % 4),)
        if t < 4:
            hh = t // 8
            dep1 = tuple(t_hqT[hh][0:11]) + tuple(t_wdT[0:11])
            dep1 = dep1 + (psum_guard.get(t % 4),)
            sched.op("pe", lambda e, t=t, ps=ps: mk_dmm(e, t, ps, 0, 11),
                     deps=dep1)
            t_mm = sched.op("pe", lambda e, t=t, ps=ps: mk_dmm(e, t, ps, 11, IC),
                            deps=tuple(t_hqT[hh][11:]) + tuple(t_wdT[11:]))
        else:
            t_mm = sched.op("pe", lambda e, t=t, ps=ps: mk_dmm(e, t, ps, 0, IC),
                            deps=deps)
        yslot = y1r[:, (t % 3) * D:(t % 3 + 1) * D]
        ydeps = (t_hq_store[NT - 1], wd_store[14], wd_store[15])
        if t >= 3:
            ydeps = ydeps + (down_evac[t - 3],)
        t_yl = sched.dma("sp", yslot, y1_dram[t * 128:(t + 1) * 128, :],
                         deps=ydeps)
        oslot = outb[:, (t % 3) * D:(t % 3 + 1) * D]
        odeps = (t_mm, t_yl, t_hq_store[NT - 1], wd_store[15])
        if t >= 3:
            odeps = odeps + (t_ostores[t - 3],)
        t_ev = sched.op("dve",
                        lambda e, ps=ps, yslot=yslot, oslot=oslot: e.tensor_tensor(
                            oslot, ps[:, :], yslot, ALU.add), deps=odeps)
        down_evac[t] = t_ev
        t_bn0 = sched.op("dve", lambda e, oslot=oslot: e.bn_stats(
            bnbuf[:, 0:6], oslot[:, 0:512]), deps=(t_ev, ln2_prev))
        t_bn = sched.op("dve", lambda e, oslot=oslot: e.bn_stats(
            bnbuf[:, 6:12], oslot[:, 512:1024]), deps=(t_bn0,))
        t_ag = sched.op("dve", lambda e: e.bn_aggr(mvst[:, 2:4], bnbuf[:]),
                        deps=(t_bn,))
        t_sq = sched.op("act", lambda e: e.activation(
            sqc[:], mvst[:, 3:4], AF.Sqrt, bias=epsc[:, 0:1]), deps=(t_ag,))
        t_rsd = sched.op("dve", lambda e: e.reciprocal(rstdc[:], sqc[:]),
                         deps=(t_sq,))
        t_nm0 = sched.op("dve", lambda e: e.tensor_scalar(
            nmr[:], mvst[:, 2:3], rstdc[:, 0:1], -1.0, ALU.mult, ALU.mult),
            deps=(t_rsd,))
        t_nm = sched.op("act", lambda e, oslot=oslot: e.activation(
            oslot, oslot, AF.Identity, bias=nmr[:, 0:1], scale=rstdc[:, 0:1]),
            deps=(t_nm0, t_sq))
        t_ostores[t] = sched.dma("sp", out_d[t * 128:(t + 1) * 128, :],
                                 oslot, deps=(t_nm,))
        ln2_prev = t_nm

    finals = sched.all_dma_barrier("sp")
    sched.finalize()
    with es, nc.Block() as block:
        @block.sync
        def _(sync):
            sched.replay("sp", sync)
            for (sname, v) in finals:
                sync.wait_ge(sched.sems[sname], v)

        @block.vector
        def _(v):
            sched.replay("dve", v)

        @block.scalar
        def _(s):
            sched.replay("act", s)

        @block.gpsimd
        def _(g):
            sched.replay("pool", g)

        @block.tensor
        def _(t):
            sched.replay("pe", t)
    return nc


_NC = None


def kernel(L0_attn_input_data, L0_attn_W1, L0_attn_W2, L0_attn_bias,
           L0_attn_embeddings_table, L0_ffn_w_gate, L0_ffn_w_up,
           L0_ffn_w_down):
    global _NC
    x = np.ascontiguousarray(np.asarray(L0_attn_input_data, dtype=np.float32))
    W1 = np.ascontiguousarray(np.asarray(L0_attn_W1, dtype=np.float32))
    W2 = np.ascontiguousarray(np.asarray(L0_attn_W2, dtype=np.float32))
    bias = np.ascontiguousarray(
        np.asarray(L0_attn_bias, dtype=np.float32)).reshape(1, D)
    et = np.ascontiguousarray(
        np.asarray(L0_attn_embeddings_table, dtype=np.float32))
    wg = np.ascontiguousarray(np.asarray(L0_ffn_w_gate, dtype=np.float32))
    wu = np.ascontiguousarray(np.asarray(L0_ffn_w_up, dtype=np.float32))
    wd = np.ascontiguousarray(np.asarray(L0_ffn_w_down, dtype=np.float32))

    if _NC is None:
        _NC = build_fused()
    ident = np.eye(21, dtype=np.float32)
    eye128 = np.eye(128, dtype=np.float32)
    ins = [{"x": x[c], "w1": W1, "w2": W2, "bias": bias, "et": et,
            "ident": ident, "eye128": eye128,
            "wg": wg, "wu": wu, "wd": wd} for c in range(NCORES)]
    res = run_bass_kernel_spmd(_NC, ins, core_ids=list(range(NCORES)))
    out = np.stack([res.results[c]["out"] for c in range(NCORES)], axis=0)
    return out.astype(np.float32)


# version 19
# speedup vs baseline: 1.0586x; 1.0579x over previous
import numpy as np
import concourse.bass as bass
import concourse.mybir as mybir
from concourse.bass_utils import run_bass_kernel_spmd
from contextlib import ExitStack

F32 = mybir.dt.float32
BF16 = mybir.dt.bfloat16
FP16 = mybir.dt.float16
ALU = mybir.AluOpType
AF = mybir.ActivationFunctionType
AX = mybir.AxisListType

NCORES = 8
N, D, I = 2048, 1024, 2816
NT = N // 128
KC = D // 128
IC = I // 128
RND_HC = 6291456.0
VELT_C = 4194305.0
PE_TAPS = list(range(1, 21))
PAD = 32
QW = 1024


class Sched:

    ENGS = ("pe", "dve", "act", "pool", "sp")
    NPOOL = {"act": 12, "pool": 8, "sp": 20}

    def __init__(self, nc):
        self.nc = nc
        self.items = {e: [] for e in self.ENGS}
        self.count = {}
        self.sems = {}
        self.rot = {e: 0 for e in self.NPOOL}

    def make_sems(self, es):
        for e in self.ENGS:
            self.sems[e] = es.enter_context(self.nc.semaphore(f"sem_{e}"))
            self.count[e] = 0
        for e, np_ in self.NPOOL.items():
            for i in range(np_):
                q = f"{e}q{i}"
                self.sems[q] = es.enter_context(self.nc.semaphore(f"sem_{q}"))
                self.count[q] = 0

    def op(self, eng, fn, deps=()):
        self.count[eng] += 1
        tok = (eng, self.count[eng])
        self.items[eng].append(
            (fn, tuple(d for d in deps if d is not None), tok, 1, eng))
        return tok

    def dma(self, eng, out, in_, deps=(), **kw):
        assert eng in self.NPOOL
        q = f"{eng}q{self.rot[eng]}"
        self.rot[eng] = (self.rot[eng] + 1) % self.NPOOL[eng]
        prev = self.count[q]
        deps = tuple(d for d in deps if d is not None)
        if prev > 0:
            deps = deps + ((q, prev),)
        self.count[q] += 16
        tok = (q, self.count[q])
        self.items[eng].append(
            (lambda e: e.dma_start(out=out, in_=in_, **kw), deps, tok, 16, q))
        return tok

    def finalize(self):
        self.prior = {}
        for eng in self.ENGS:
            run = {}
            for (fn, deps, tok, inc, sem) in self.items[eng]:
                self.prior[tok] = dict(run)
                run[sem] = tok[1]

    def replay(self, eng, e):
        waited = {}
        for (fn, deps, tok, inc, sem) in self.items[eng]:
            want = {}
            for (s, v) in deps:
                if s == eng:
                    v = min(v, self.prior[tok].get(s, 0))
                if v > want.get(s, 0):
                    want[s] = v
            for s, v in want.items():
                if v > waited.get(s, 0):
                    e.wait_ge(self.sems[s], v)
                    waited[s] = v
            inst = fn(e)
            if inst is not None:
                inst.then_inc(self.sems[sem], inc)

    def all_dma_barrier(self, eng):
        return tuple((f"{eng}q{i}", self.count[f"{eng}q{i}"])
                     for i in range(self.NPOOL[eng])
                     if self.count[f"{eng}q{i}"] > 0)


class Arena:
    def __init__(self, nc, total_bytes):
        self.nc = nc
        self.total = total_bytes
        self.handle = None
        self.named = []

    def reserve(self, es):
        self.handle = es.enter_context(
            self.nc.sbuf_tensor("arena", [128, self.total], mybir.dt.uint8))

    def at(self, name, offset, shape, dtype):
        per_part = int(np.prod(shape[1:])) * mybir.dt.size(dtype)
        assert offset % 32 == 0, (name, offset)
        assert offset + per_part <= self.total, (name, offset, per_part)
        h = self.nc.virtual_sbuf_tensor(name, shape, dtype)
        self.named.append((h, offset))
        return h

    def finalize(self):
        base = self.nc.lookup_mloc(self.handle).addr
        for h, off in self.named:
            self.nc._tensor(h.name, h.shape, h.dtype, type="SB")
            mloc = self.nc.lookup_mloc(h)
            mloc.allocated = not self.nc.target_bir_lowering
            mloc.addr = base + off


MAP_P = {"u": "pool", "w": "pool", "hi": "pool", "G1": "pool", "q": "pool"}
MAP_D = {"u": "pool", "w": "dve", "hi": "dve", "G1": "pool", "q": "dve"}


def qv_stages(sched, bufs, src_ap, q_out_ap, width, src_tok,
              extra_q_deps, prev1, prev2, emap, T):
    nb = width // 16
    amax, s_t, rs_t = bufs["amax"], bufs["s"], bufs["rs"]
    u, cu, w = bufs["u"], bufs["cu"], bufs["w"]
    hi, Ra, Rb, lcp = bufs["hi"], bufs["Ra"], bufs["Rb"], bufs["lcp"]
    p1 = prev1 or {}
    p2 = prev2 or {}

    def g(d, *keys):
        return tuple(d[k] for k in keys if d.get(k) is not None)

    src_deps = tuple(src_tok) if isinstance(src_tok, (tuple, list)) and \
        src_tok and isinstance(src_tok[0], tuple) else (src_tok,)
    src3 = src_ap.rearrange("p (b k) -> p b k", k=16)
    rs_b = rs_t[:, :nb].unsqueeze(2).broadcast_to([128, nb, 16])
    s_b = s_t[:, :nb].unsqueeze(2).broadcast_to([128, nb, 16])
    u3 = u[:, :width].rearrange("p (b k) -> p b k", k=16)
    uf, cuf, wf = u[:, :width], cu[:, :width], w[:, :width]
    hif, raf, rbf, lcpf = (hi[:, :width], Ra[:, :width], Rb[:, :width],
                           lcp[:, :width])
    ctok = tuple(bufs.get("ctok") or ())
    cC = bufs["cC"][:, 0:1]
    cH = bufs["cH"][:, 0:1]
    cn1 = bufs["cn1"][:, 0:1]
    q3 = q_out_ap.rearrange("p (b k) -> p b k", k=16)
    hi3 = hif.rearrange("p (b k) -> p b k", k=16)

    def st_am():
        T["am"] = sched.op("dve", lambda e: e.tensor_reduce(
            amax[:, :nb], src3, AX.X, ALU.max, apply_absolute_value=True),
            deps=src_deps + g(p2, "s"))

    def st_s():
        T["s"] = sched.op("dve", lambda e: e.tensor_scalar(
            s_t[:, :nb], amax[:, :nb], 1.0 / 6.0, 1e-12, ALU.mult, ALU.max),
            deps=(T["am"],) + g(p2, "q", "rs"))

    def st_rs():
        T["rs"] = sched.op("dve", lambda e: e.reciprocal(
            rs_t[:, :nb], s_t[:, :nb]), deps=(T["s"],) + g(p2, "u"))

    def st_u():
        T["u"] = sched.op(emap["u"], lambda e: e.tensor_tensor(
            u3, src3, rs_b, ALU.mult), deps=(T["rs"],) + g(p1, "b5p"))

    def st_cu():
        T["cu"] = sched.op("act", lambda e: e.activation(
            cuf, uf, AF.Identity, scale=cC),
            deps=(T["u"],) + g(p1, "lcp") + ctok)

    def st_w():
        T["w"] = sched.op(emap["w"], lambda e: e.tensor_tensor(
            wf, cuf, uf, ALU.subtract), deps=(T["cu"],) + g(p1, "G"))

    def st_hi():
        T["hi"] = sched.op(emap["hi"], lambda e: e.tensor_tensor(
            hif, cuf, wf, ALU.subtract), deps=(T["w"],) + g(p1, "q"))

    def st_b5p():
        T["b5p"] = sched.op("act", lambda e: e.activation(
            cuf, uf, AF.Identity, bias=cH), deps=(T["hi"],))

    def st_Ra():
        T["Ra"] = sched.op("act", lambda e: e.activation(
            raf, hif, AF.Relu, bias=cn1), deps=(T["hi"],))

    def st_Rb():
        T["Rb"] = sched.op("act", lambda e: e.activation(
            rbf, hif, AF.Relu, bias=cn1, scale=cn1), deps=(T["Ra"],))

    def st_lcp():
        T["lcp"] = sched.op("dve", lambda e: e.tensor_scalar(
            lcpf, cuf, RND_HC, -1.0, ALU.subtract, ALU.max),
            deps=(T["b5p"],))

    def st_G1():
        T["G1"] = sched.op(emap["G1"], lambda e: e.tensor_tensor(
            raf, raf, rbf, ALU.subtract), deps=(T["Rb"],))

    def st_G():
        T["G"] = sched.op("dve", lambda e: e.scalar_tensor_tensor(
            hif, lcpf, 1.0, raf, ALU.min, ALU.add),
            deps=(T["lcp"], T["G1"]))

    def st_q():
        T["q"] = sched.op(emap["q"], lambda e: e.tensor_tensor(
            q3, hi3, s_b, ALU.mult), deps=(T["G"],) + tuple(extra_q_deps))

    return [st_am, st_s, st_rs, st_u, st_cu, st_w, st_hi, st_b5p,
            st_Ra, st_Rb, st_lcp, st_G1, st_G, st_q]


def emit_quant_v(sched, bufs, src_ap, q_out_ap, width, src_tok,
                 extra_q_deps=(), prev1=None, prev2=None, q_on_dve=False,
                 emap=None):
    T = {}
    emap = emap or (dict(MAP_P, q="dve") if q_on_dve else MAP_P)
    for st in qv_stages(sched, bufs, src_ap, q_out_ap, width, src_tok,
                        extra_q_deps, prev1, prev2, emap, T):
        st()
    return T


def emit_quant_pair(sched, specs):
    Ts = [dict() for _ in specs]
    plans = [qv_stages(sched, b, s, q, wd, tok, ex, p1, p1, em, Ts[i])
             for i, (b, s, q, wd, tok, ex, p1, em) in enumerate(specs)]
    for si in range(len(plans[0])):
        for p in plans:
            p[si]()
    return Ts


def build_fused():
    nc = bass.Bass("TRN2", target_bir_lowering=False, debug=False)
    x_d = nc.dram_tensor("x", [N, D], F32, kind="ExternalInput").ap()
    w1_d = nc.dram_tensor("w1", [D, D], F32, kind="ExternalInput").ap()
    w2_d = nc.dram_tensor("w2", [D, D], F32, kind="ExternalInput").ap()
    bias_d = nc.dram_tensor("bias", [1, D], F32, kind="ExternalInput").ap()
    et_d = nc.dram_tensor("et", [21, D], F32, kind="ExternalInput").ap()
    id_d = nc.dram_tensor("ident", [21, 21], F32, kind="ExternalInput").ap()
    eye_d = nc.dram_tensor("eye128", [128, 128], F32, kind="ExternalInput").ap()
    wg_d = nc.dram_tensor("wg", [I, D], F32, kind="ExternalInput").ap()
    wu_d = nc.dram_tensor("wu", [I, D], F32, kind="ExternalInput").ap()
    wd_d = nc.dram_tensor("wd", [D, I], F32, kind="ExternalInput").ap()
    out_d = nc.dram_tensor("out", [N, D], F32, kind="ExternalOutput").ap()

    xb_dram = nc.dram_tensor("xb_dram", [N, D], FP16, kind="Internal").ap()
    y1_dram = nc.dram_tensor("y1_dram", [N, D], F32, kind="Internal").ap()
    y1q_dram = nc.dram_tensor("y1q_dram", [N, D], BF16, kind="Internal").ap()
    hq_dram = nc.dram_tensor("hq_dram", [N, I], BF16, kind="Internal").ap()
    wgq_d = nc.dram_tensor("wgq", [I, D], BF16, kind="Internal").ap()
    wuq_d = nc.dram_tensor("wuq", [I, D], BF16, kind="Internal").ap()
    wdq_d = nc.dram_tensor("wdq", [D, I], BF16, kind="Internal").ap()

    es = ExitStack()
    sched = Sched(nc)
    sched.make_sems(es)

    AR = 207 * 1024
    ar = Arena(nc, AR)
    ar.reserve(es)

    y1qT = ar.at("y1qT", 0, [128, KC * N], BF16)
    convA = ar.at("convA", 0, [128, N], FP16)
    convB = ar.at("convB", 4096, [128, N], FP16)
    convC = ar.at("convC", 8192, [128, N], FP16)
    convD = ar.at("convD", 12288, [128, N], FP16)
    convE = ar.at("convE", 16384, [128, N], FP16)
    et_rows = ar.at("et_rows", 20480, [21, D], F32)
    ident21 = ar.at("ident21", 24576, [21, 21], F32)

    GEN = 32768
    x_res = ar.at("x_res", GEN, [128, NT * D], F32)
    xT = ar.at("xT", GEN + 65536, [128, KC * (N + PAD)], FP16)
    hT = ar.at("hT", GEN + 98816, [128, KC * N], FP16)
    W1b = ar.at("W1b", GEN + 131584, [128, KC * D], FP16)
    W2b = ar.at("W2b", GEN + 148096, [128, KC * D], FP16)

    def mk_qset(tag, W, u_o, cu_o, w_o, hi_o, ra_o, rb_o, am_o, s_o, rs_o,
                src_o=None, q_o=None):
        d = {"u": ar.at(f"{tag}_u", u_o, [128, W], F32),
             "cu": ar.at(f"{tag}_cu", cu_o, [128, W], F32),
             "w": ar.at(f"{tag}_w", w_o, [128, W], F32),
             "lcp": ar.at(f"{tag}_lcp", w_o, [128, W], BF16),
             "hi": ar.at(f"{tag}_hi", hi_o, [128, W], BF16),
             "Ra": ar.at(f"{tag}_Ra", ra_o, [128, W], BF16),
             "Rb": ar.at(f"{tag}_Rb", rb_o, [128, W], BF16),
             "amax": ar.at(f"{tag}_am", am_o, [128, W // 16], F32),
             "s": ar.at(f"{tag}_s", s_o, [128, W // 16], F32),
             "rs": ar.at(f"{tag}_rs", rs_o, [128, W // 16], F32)}
        if src_o is not None:
            d["src"] = ar.at(f"{tag}_src", src_o, [128, W], F32)
        if q_o is not None:
            d["qout"] = ar.at(f"{tag}_q", q_o, [128, W], BF16)
        return d

    vw2 = [mk_qset("vw0", 1024, 49152, 53248, 57344, 61440, 63488, 65536,
                   69632, 69888, 70144, src_o=45056, q_o=67584),
           mk_qset("vw1", 1024, 74496, 78592, 82688, 86784, 88832, 90880,
                   94976, 95232, 95488, src_o=70400, q_o=92928)]

    vy1 = {
        "u": ar.at("vy_u", 197248, [128, QW], F32),
        "cu": ar.at("vy_cu", 201344, [128, QW], F32),
        "w": ar.at("vy_w", 16384, [128, QW], F32),
        "lcp": ar.at("vy_lcp", 16384, [128, QW], BF16),
        "hi": ar.at("vy_hi", 2048, [128, QW], BF16),
        "Ra": ar.at("vy_Ra", 6144, [128, QW], BF16),
        "Rb": ar.at("vy_Rb", 10240, [128, QW], BF16),
        "amax": ar.at("vy_amax", 205440, [128, QW // 16], F32),
        "s": ar.at("vy_s", 205728, [128, QW // 16], F32),
        "rs": ar.at("vy_rs", 206016, [128, QW // 16], F32),
    }
    y1q_st = [ar.at("y1qs0", 26624, [128, D], BF16),
              ar.at("y1qs1", 30720, [128, D], BF16)]
    diag16 = ar.at("diag16", 197248, [128, len(PE_TAPS) * 128], FP16)

    wgT = ar.at("wgT", 32768, [128, KC * I], BF16)
    wuT = ar.at("wuT", 77824, [128, KC * I], BF16)
    vh2 = [mk_qset("vh0", 1408, 122880, 134144, 151040, 145408, 175776,
                   181408, 162304, 163040, 163776),
           mk_qset("vh1", 1408, 128512, 139776, 156672, 148224, 178592,
                   184224, 162656, 163392, 164128)]
    h_t = ar.at("h_t", 164512, [128, I], F32)
    hq_t = ar.at("hq_t", 187040, [128, I], BF16)
    wd_src = ar.at("wd_src", 192672, [128, 1408], F32)
    wd_src2 = ar.at("wd_src2", 198304, [128, 1408], F32)
    sg_b = [None] * 4
    up_b = [None] * 4
    cur2 = 199456
    for i in range(4):
        sg_b[i] = ar.at(f"sg{i}", cur2, [128, 512], BF16); cur2 += 1024
        up_b[i] = ar.at(f"up{i}", cur2, [128, 512], BF16); cur2 += 1024
    assert cur2 <= 208160, cur2

    hqT = ar.at("hqT", 32768, [128, IC * N], BF16)
    wdT = ar.at("wdT", 122880, [128, IC * D], BF16)
    y1r = ar.at("y1r", 167936, [128, 3 * D], F32)
    outb = ar.at("outb", 180224, [128, 3 * D], F32)

    sm = 208160
    et_fm = ar.at("et_fm", sm, [128, KC * 21], F32); sm += KC * 21 * 4
    sm = (sm + 31) // 32 * 32
    biasb = ar.at("biasb", sm, [1, D], BF16); sm += D * 2
    onesb = ar.at("onesb", sm, [1, 128], BF16); sm += 256
    epsc = ar.at("epsc", sm, [128, 1], F32); sm += 32
    mvst = ar.at("mvst", sm, [128, 8], F32); sm += 32
    bnbuf = ar.at("bnbuf", sm, [128, 12], F32); sm += 64
    rstdc = ar.at("rstdc", sm, [128, 1], F32); sm += 32
    sqc = ar.at("sqc", sm, [128, 1], F32); sm += 32
    nmr = ar.at("nmr", sm, [128, 1], F32); sm += 32
    eye16 = ar.at("eye16", sm, [128, 128], FP16); sm += 256
    cvelt = ar.at("cvelt", sm, [128, 1], F32); sm += 32
    crndhc = ar.at("crndhc", sm, [128, 1], F32); sm += 32
    cneg1 = ar.at("cneg1", sm, [128, 1], F32); sm += 32
    assert sm <= AR, sm

    att_ps = [es.enter_context(nc.psum_tensor(f"aps{i}", [128, 1024], F32))
              for i in range(4)]

    ar.finalize()

    t_ep = sched.op("dve", lambda e: e.memset(epsc[:], 1e-5))
    t_ones = sched.op("dve", lambda e: e.memset(onesb[:], 1.0))
    ctok = (sched.op("dve", lambda e: e.memset(cvelt[:], VELT_C)),
            sched.op("dve", lambda e: e.memset(crndhc[:], RND_HC)),
            sched.op("dve", lambda e: e.memset(cneg1[:], -1.0)))
    consts = {"cC": cvelt, "cH": crndhc, "cn1": cneg1, "ctok": ctok}
    for b in vw2 + vh2 + [vy1]:
        b.update(consts)
    t_pad = [sched.op("dve", lambda e, k=k: e.memset(
        xT[:, k * (N + PAD): k * (N + PAD) + PAD], 0.0)) for k in range(KC)]

    t_xres = [None] * NT
    for t in range(3):
        t_xres[t] = sched.dma("sp", x_res[:, t * D:(t + 1) * D],
                              x_d[t * 128:(t + 1) * 128, :])

    wjobs = []
    for src_d, dst_d in ((wg_d, wgq_d), (wu_d, wuq_d)):
        for j in range(IC):
            wjobs.append((src_d[j * 128:(j + 1) * 128, :],
                          dst_d[j * 128:(j + 1) * 128, :]))
    NWJ = len(wjobs)
    wload = [None] * NWJ
    wstore = [None] * NWJ
    wq_prev = [None, None]
    wload[0] = sched.dma("sp", vw2[0]["src"][:, :], wjobs[0][0])
    wload[1] = sched.dma("sp", vw2[1]["src"][:, :], wjobs[1][0])

    def emit_wpair(j):
        specs = []
        for o, em in ((0, MAP_P), (1, MAP_D)):
            p = (j + o) % 2
            S = vw2[p]
            ex = (wstore[j + o - 2],) if j + o >= 2 else ()
            specs.append((S, S["src"][:, :], S["qout"][:, :], 1024,
                          (wload[j + o],), ex, wq_prev[p], em))
        Ts = emit_quant_pair(sched, specs)
        for o in (0, 1):
            p = (j + o) % 2
            wq_prev[p] = Ts[o]
            wstore[j + o] = sched.dma("sp", wjobs[j + o][1],
                                      vw2[p]["qout"][:, :],
                                      deps=(Ts[o]["q"],))
            if j + o + 2 < NWJ:
                wload[j + o + 2] = sched.dma("sp", vw2[p]["src"][:, :],
                                             wjobs[j + o + 2][0],
                                             deps=(Ts[o]["u"],))

    t_xb = sched.dma("pool", xb_dram[:], x_d[:])
    t_xT = [sched.dma("sp", xT[:, k * (N + PAD) + PAD:(k + 1) * (N + PAD)],
                      xb_dram[:, k * 128:(k + 1) * 128],
                      deps=(t_xb, t_pad[k]), transpose=True) for k in range(KC)]
    t_w1 = [sched.dma("pool", W1b[:, k * D:(k + 1) * D],
                      w1_d[k * 128:(k + 1) * 128, :]) for k in range(KC)]
    t_w2 = [sched.dma("pool", W2b[:, k * D:(k + 1) * D],
                      w2_d[k * 128:(k + 1) * 128, :]) for k in range(KC)]
    t_bias = sched.dma("pool", biasb[:], bias_d[:])
    t_eye = sched.dma("pool", eye16[:], eye_d[:])
    t_etl = sched.dma("sp", et_rows[:], et_d[:])
    t_id = [sched.dma("sp", ident21[:], id_d[:])]
    t_etfm = []
    for k in range(KC):
        t_tr = sched.op("pe", lambda e, k=k: e.transpose(
            att_ps[0][:, 0:21], et_rows[0:21, k * 128:(k + 1) * 128], ident21[:]),
            deps=tuple(t_id) + (t_etl, t_etfm[-1] if t_etfm else None))
        t_cp = sched.op("dve", lambda e, k=k: e.tensor_copy(
            et_fm[:, k * 21:(k + 1) * 21], att_ps[0][:, 0:21]), deps=(t_tr,))
        t_etfm.append(t_cp)
    t_et_done = t_etfm[-1]

    def xsh(k, j):
        base = k * (N + PAD) + PAD - j
        return xT[:, base: base + N]

    def etc(k, j):
        return et_fm[:, k * 21 + j: k * 21 + j + 1]

    conv_done = []
    conv_ps = [att_ps[3][:, 0:512], att_ps[3][:, 512:1024]]
    cp_guard = [None, None]
    cpar = [0]
    wj_next = [0]
    for k in range(KC):
        npe = len(PE_TAPS)
        eye_b = eye16[:].unsqueeze(1).broadcast_to([128, npe, 128])
        etv = et_fm[:, k * 21 + PE_TAPS[0]: k * 21 + PE_TAPS[0] + npe]
        et_b = etv.unsqueeze(2).broadcast_to([128, npe, 128])
        t_dg = sched.op("dve", lambda e, k=k, eye_b=eye_b, et_b=et_b: e.tensor_tensor(
            diag16[:].rearrange("p (j m) -> p j m", m=128), eye_b, et_b,
            ALU.mult), deps=(t_et_done, t_eye) +
            (tuple(conv_done[-1]) if conv_done else ()))
        pe_evacs = []
        for nt in range(4):
            par = cpar[0] % 2
            cpar[0] += 1

            def mk_conv(e, k=k, nt=nt, par=par):
                last = None
                for ji, j in enumerate(PE_TAPS):
                    e.ldweights(diag16[:, ji * 128:(ji + 1) * 128])
                    last = e.matmul(
                        conv_ps[par], diag16[:, ji * 128:(ji + 1) * 128],
                        xT[:, k * (N + PAD) + PAD - j + nt * 512:
                           k * (N + PAD) + PAD - j + nt * 512 + 512],
                        start=(ji == 0), stop=(ji == len(PE_TAPS) - 1))
                return last
            cdeps = (t_dg, t_xT[k])
            if cp_guard[par] is not None:
                cdeps = cdeps + (cp_guard[par],)
            t_cmm = sched.op("pe", mk_conv, deps=cdeps)
            t_evc = sched.op("act", lambda e, k=k, nt=nt, par=par: e.activation(
                hT[:, k * N + nt * 512: k * N + (nt + 1) * 512],
                conv_ps[par], AF.Identity), deps=(t_cmm,))
            cp_guard[par] = t_evc
            pe_evacs.append(t_evc)
        conv_done.append(tuple(pe_evacs))
        for _ in range(3):
            if wj_next[0] < NWJ:
                emit_wpair(wj_next[0])
                wj_next[0] += 2
    conv_done = [t for evs in conv_done for t in evs]
    assert wj_next[0] == NWJ
    for t in range(3, NT):
        t_xres[t] = sched.dma("sp", x_res[:, t * D:(t + 1) * D],
                              x_d[t * 128:(t + 1) * 128, :],
                              deps=(wstore[NWJ - 1], wstore[NWJ - 2]))

    def mk_mm(e, t, ps):
        last = None
        for k in range(KC):
            stat = xT[:, k * (N + PAD) + PAD + t * 128:
                      k * (N + PAD) + PAD + (t + 1) * 128]
            e.ldweights(stat)
            e.matmul(ps[:, 0:512], stat, W1b[:, k * D: k * D + 512],
                     start=(k == 0), stop=False)
            last = e.matmul(ps[:, 512:1024], stat,
                            W1b[:, k * D + 512:(k + 1) * D],
                            start=(k == 0), stop=False)
        return last

    def mk_mm2(e, t, ps):
        for k in range(KC):
            stat = hT[:, k * N + t * 128: k * N + (t + 1) * 128]
            e.ldweights(stat)
            e.matmul(ps[:, 0:512], stat, W2b[:, k * D: k * D + 512],
                     start=False, stop=False)
            e.matmul(ps[:, 512:1024], stat, W2b[:, k * D + 512:(k + 1) * D],
                     start=False, stop=False)
        e.ldweights(onesb[0:1, :])
        e.matmul(ps[:, 0:512], onesb[0:1, :], biasb[0:1, 0:512],
                 start=False, stop=True)
        return e.matmul(ps[:, 512:1024], onesb[0:1, :], biasb[0:1, 512:1024],
                        start=False, stop=True)

    attn_evac = [None] * NT
    tok_x = [None] * NT
    t_y1d = [None] * NT
    ln_prev = None
    y1q_tok = [None] * NT
    y1q_dump = [None] * NT
    qprev = {}
    prev_dump = {0: None, 1: None}

    for t in range(3):
        tok_x[t] = sched.op("pe", lambda e, t=t: mk_mm(e, t, att_ps[t % 3]),
                            deps=tuple(t_xT) + tuple(t_w1) + (t_et_done,))
    for t in range(NT):
        ps = att_ps[t % 3]
        tok_h = sched.op("pe", lambda e, t=t, ps=ps: mk_mm2(e, t, ps),
                         deps=tuple(conv_done) + tuple(t_w2) +
                         (tok_x[t], t_ones, t_bias))
        t_ev = sched.op("dve", lambda e, t=t, ps=ps: e.tensor_tensor(
            x_res[:, t * D:(t + 1) * D], ps[:, :], x_res[:, t * D:(t + 1) * D],
            ALU.add), deps=(tok_h, t_xres[t]))
        attn_evac[t] = t_ev
        if t + 3 < NT:
            tok_x[t + 3] = sched.op(
                "pe", lambda e, t=t: mk_mm(e, t + 3, att_ps[(t + 3) % 3]),
                deps=(t_ev,))
        t_bn0 = sched.op("dve", lambda e, t=t: e.bn_stats(
            bnbuf[:, 0:6], x_res[:, t * D: t * D + 512]), deps=(t_ev, ln_prev))
        t_bn = sched.op("dve", lambda e, t=t: e.bn_stats(
            bnbuf[:, 6:12], x_res[:, t * D + 512:(t + 1) * D]), deps=(t_bn0,))
        t_ag = sched.op("dve", lambda e: e.bn_aggr(mvst[:, 0:2], bnbuf[:]),
                        deps=(t_bn,))
        t_sq = sched.op("act", lambda e: e.activation(
            sqc[:], mvst[:, 1:2], AF.Sqrt, bias=epsc[:, 0:1]), deps=(t_ag, t_ep))
        t_rsd = sched.op("dve", lambda e: e.reciprocal(rstdc[:], sqc[:]),
                         deps=(t_sq,))
        t_nm0 = sched.op("dve", lambda e: e.tensor_scalar(
            nmr[:], mvst[:, 0:1], rstdc[:, 0:1], -1.0, ALU.mult, ALU.mult),
            deps=(t_rsd,))
        t_nm = sched.op("act", lambda e, t=t: e.activation(
            x_res[:, t * D:(t + 1) * D], x_res[:, t * D:(t + 1) * D],
            AF.Identity, bias=nmr[:, 0:1], scale=rstdc[:, 0:1]),
            deps=(t_nm0, t_sq))
        ln_prev = t_nm
        t_y1d[t] = sched.dma("sp", y1_dram[t * 128:(t + 1) * 128, :],
                             x_res[:, t * D:(t + 1) * D], deps=(t_nm,))
        stg = y1q_st[t % 2]
        qdeps = ((prev_dump[t % 2],) if prev_dump[t % 2] else ())
        Tq = emit_quant_v(sched, vy1, x_res[:, t * D:(t + 1) * D],
                          stg[:], QW, (t_nm,), extra_q_deps=qdeps,
                          prev1=qprev.get(t - 1), prev2=qprev.get(t - 1))
        qprev[t] = Tq
        y1q_tok[t] = Tq["q"]
        y1q_dump[t] = sched.dma("sp", y1q_dram[t * 128:(t + 1) * 128, :],
                                stg[:], deps=(Tq["q"],))
        prev_dump[t % 2] = y1q_dump[t]

    y1qT_half = [None, None]
    for half in range(2):
        lst = []
        for k in range(KC):
            dd = tuple(y1q_dump[half * 8:(half + 1) * 8])
            if (half == 0 and k == 4) or (half == 1 and k in (0, 1, 2, 4)):
                dd = dd + (y1q_tok[NT - 1],)
            lst.append(sched.dma(
                "sp", y1qT[:, k * N + half * 1024: k * N + half * 1024 + 1024],
                y1q_dram[half * 1024:(half + 1) * 1024, k * 128:(k + 1) * 128],
                deps=dd, transpose=True))
        y1qT_half[half] = lst

    def xdead(byte0, byte1):
        t0 = max(0, (byte0 - 32768) // 4096)
        t1 = min(NT - 1, (byte1 - 1 - 32768) // 4096)
        dd = []
        for tt in range(t0, t1 + 1):
            dd += [y1q_tok[tt], t_y1d[tt]]
        return tuple(dd)
    wsts = tuple(wstore)
    t_wgT = [sched.dma("sp", wgT[:, k * I:(k + 1) * I],
                       wgq_d[:, k * 128:(k + 1) * 128], transpose=True,
                       deps=xdead(32768 + k * 5632, 32768 + (k + 1) * 5632) +
                       wsts)
             for k in range(KC)]
    t_wuT = [sched.dma("sp", wuT[:, k * I:(k + 1) * I],
                       wuq_d[:, k * 128:(k + 1) * 128], transpose=True,
                       deps=xdead(77824 + k * 5632, 77824 + (k + 1) * 5632) +
                       ((attn_evac[NT - 1],) if k >= 4 else ()) + wsts)
             for k in range(KC)]

    IBANKS = [(ib * 512, min(512, I - ib * 512)) for ib in range((I + 511) // 512)]
    t_hq_store = [None] * NT
    wd_store = [None] * 16
    psum_guard = {}
    h_guard = [()]
    range_prev = [None, None]
    chain2 = [None, None]
    tog = [0]
    h_parts_of = [None] * NT
    pslot = [0]

    def emit_ffn_tile(t):
        h_parts = []
        for bi, (i0, w) in enumerate(IBANKS):
            ps = att_ps[pslot[0] % 4]
            sg = sg_b[pslot[0] % 4]
            uv = up_b[pslot[0] % 4]
            deps = tuple(t_wgT) + tuple(t_wuT) + tuple(y1qT_half[t // 8])
            if psum_guard.get(pslot[0] % 4) is not None:
                deps = deps + (psum_guard[pslot[0] % 4],)
            pslot[0] += 1

            def mk_gmm(e, t=t, i0=i0, w=w, ps=ps):
                last = None
                for k in range(KC):
                    stat = y1qT[:, k * N + t * 128: k * N + (t + 1) * 128]
                    e.ldweights(stat)
                    e.matmul(ps[:, 0:w], stat, wgT[:, k * I + i0: k * I + i0 + w],
                             start=(k == 0), stop=(k == KC - 1))
                    last = e.matmul(ps[:, 512:512 + w], stat,
                                    wuT[:, k * I + i0: k * I + i0 + w],
                                    start=(k == 0), stop=(k == KC - 1))
                return last
            t_mm = sched.op("pe", mk_gmm, deps=deps)
            t_sg = sched.op("act", lambda e, sg=sg, ps=ps, w=w: e.activation(
                sg[:, 0:w], ps[:, 0:w], AF.Silu), deps=(t_mm,))
            t_uv = sched.op("act", lambda e, uv=uv, ps=ps, w=w: e.activation(
                uv[:, 0:w], ps[:, 512:512 + w], AF.Identity), deps=(t_sg,))
            t_hm = sched.op("dve",
                            lambda e, sg=sg, uv=uv, i0=i0, w=w:
                            e.tensor_tensor(h_t[:, i0:i0 + w], sg[:, 0:w],
                                            uv[:, 0:w], ALU.mult),
                            deps=(t_uv,) + h_guard[0])
            psum_guard[(pslot[0] - 1) % 4] = t_hm
            h_parts.append(t_hm)
        h_parts_of[t] = h_parts

    def emit_hquant(t):
        specs = []
        ps = []
        for half, em in ((0, MAP_P), (1, MAP_D)):
            p = tog[0] % 2
            tog[0] += 1
            ps.append(p)
            S = vh2[p]
            c0 = half * 1408
            ex = (range_prev[half],) if range_prev[half] else ()
            specs.append((S, h_t[:, c0:c0 + 1408], hq_t[:, c0:c0 + 1408],
                          1408, tuple(h_parts_of[t]), ex, chain2[p], em))
        Ts = emit_quant_pair(sched, specs)
        for half in range(2):
            chain2[ps[half]] = Ts[half]
        h_guard[0] = (Ts[0]["u"], Ts[1]["u"])
        t_st = sched.dma("sp", hq_dram[t * 128:(t + 1) * 128, :],
                         hq_t[0:128, :], deps=(Ts[0]["q"], Ts[1]["q"]))
        t_hq_store[t] = t_st
        range_prev[0] = t_st
        range_prev[1] = t_st

    wd_jobs = []
    for i in range(8):
        for c in (0, 1408):
            wd_jobs.append((wd_d[i * 128:(i + 1) * 128, c:c + 1408],
                            wdq_d[i * 128:(i + 1) * 128, c:c + 1408]))
    wd_load = [None] * 16
    wd_T = [None] * 16

    def emit_wdpair(i):
        specs = []
        ps = []
        for o, em in ((0, MAP_P), (1, MAP_D)):
            prevT = wd_T[i + o - 2] if i + o - 2 >= 0 else None
            ld = (prevT["u"],) if prevT else (y1q_tok[NT - 1],)
            wd_load[i + o] = sched.dma("sp", (wd_src, wd_src2)[o][:, :],
                                       wd_jobs[i + o][0], deps=ld)
            p = tog[0] % 2
            tog[0] += 1
            ps.append(p)
            S = vh2[p]
            c0 = o * 1408
            ex = (range_prev[o],) if range_prev[o] else ()
            specs.append((S, (wd_src, wd_src2)[o][:, :],
                          hq_t[:, c0:c0 + 1408], 1408, (wd_load[i + o],),
                          ex, chain2[p], em))
        Ts = emit_quant_pair(sched, specs)
        for o in (0, 1):
            chain2[ps[o]] = Ts[o]
            wd_T[i + o] = Ts[o]
            wd_store[i + o] = sched.dma(
                "sp", wd_jobs[i + o][1], hq_t[:, o * 1408:(o + 1) * 1408],
                deps=(Ts[o]["q"],))
            range_prev[o] = wd_store[i + o]

    wd_next = [0]
    for t in range(NT):
        emit_ffn_tile(t)
        emit_hquant(t)
        if t >= 8 and wd_next[0] < 16:
            emit_wdpair(wd_next[0])
            wd_next[0] += 2
    assert wd_next[0] == 16

    wdsts = tuple(wd_store)
    ffa_done = tuple(h_parts_of[NT - 1])
    t_hqT = []
    for half in range(2):
        t_hqT.append([sched.dma(
            "sp", hqT[:, ic * N + half * 1024: ic * N + half * 1024 + 1024],
            hq_dram[half * 1024:(half + 1) * 1024, ic * 128:(ic + 1) * 128],
            transpose=True,
            deps=tuple(tk for tk in t_hq_store[half * 8:(half + 1) * 8] if tk) +
            ffa_done)
            for ic in range(IC)])
    t_wdT = [sched.dma("sp", wdT[:, ic * D:(ic + 1) * D],
                       wdq_d[:, ic * 128:(ic + 1) * 128], transpose=True,
                       deps=tuple(t_hq_store) + wdsts) for ic in range(IC)]

    def mk_dmm(e, t, ps, ic0, ic1):
        last = None
        for ic in range(ic0, ic1):
            stat = hqT[:, ic * N + t * 128: ic * N + (t + 1) * 128]
            e.ldweights(stat)
            e.matmul(ps[:, 0:512], stat, wdT[:, ic * D: ic * D + 512],
                     start=(ic == 0), stop=(ic == IC - 1))
            last = e.matmul(ps[:, 512:1024], stat,
                            wdT[:, ic * D + 512:(ic + 1) * D],
                            start=(ic == 0), stop=(ic == IC - 1))
        return last

    down_evac = [None] * NT
    t_ostores = [None] * NT
    ln2_prev = None
    for t in range(NT):
        ps = att_ps[t % 4]
        deps = tuple(t_hqT[t // 8]) + tuple(t_wdT)
        if t >= 4:
            deps = deps + (down_evac[t - 4],)
        else:
            deps = deps + (psum_guard.get(t % 4),)
        if t < 4:
            hh = t // 8
            dep1 = tuple(t_hqT[hh][0:11]) + tuple(t_wdT[0:11])
            dep1 = dep1 + (psum_guard.get(t % 4),)
            sched.op("pe", lambda e, t=t, ps=ps: mk_dmm(e, t, ps, 0, 11),
                     deps=dep1)
            t_mm = sched.op("pe", lambda e, t=t, ps=ps: mk_dmm(e, t, ps, 11, IC),
                            deps=tuple(t_hqT[hh][11:]) + tuple(t_wdT[11:]))
        else:
            t_mm = sched.op("pe", lambda e, t=t, ps=ps: mk_dmm(e, t, ps, 0, IC),
                            deps=deps)
        yslot = y1r[:, (t % 3) * D:(t % 3 + 1) * D]
        ydeps = (t_hq_store[NT - 1], wd_store[14], wd_store[15])
        if t >= 3:
            ydeps = ydeps + (down_evac[t - 3],)
        t_yl = sched.dma("sp", yslot, y1_dram[t * 128:(t + 1) * 128, :],
                         deps=ydeps)
        oslot = outb[:, (t % 3) * D:(t % 3 + 1) * D]
        odeps = (t_mm, t_yl, t_hq_store[NT - 1], wd_store[15])
        if t >= 3:
            odeps = odeps + (t_ostores[t - 3],)
        t_ev = sched.op("dve",
                        lambda e, ps=ps, yslot=yslot, oslot=oslot: e.tensor_tensor(
                            oslot, ps[:, :], yslot, ALU.add), deps=odeps)
        down_evac[t] = t_ev
        t_bn0 = sched.op("dve", lambda e, oslot=oslot: e.bn_stats(
            bnbuf[:, 0:6], oslot[:, 0:512]), deps=(t_ev, ln2_prev))
        t_bn = sched.op("dve", lambda e, oslot=oslot: e.bn_stats(
            bnbuf[:, 6:12], oslot[:, 512:1024]), deps=(t_bn0,))
        t_ag = sched.op("dve", lambda e: e.bn_aggr(mvst[:, 2:4], bnbuf[:]),
                        deps=(t_bn,))
        t_sq = sched.op("act", lambda e: e.activation(
            sqc[:], mvst[:, 3:4], AF.Sqrt, bias=epsc[:, 0:1]), deps=(t_ag,))
        t_rsd = sched.op("dve", lambda e: e.reciprocal(rstdc[:], sqc[:]),
                         deps=(t_sq,))
        t_nm0 = sched.op("dve", lambda e: e.tensor_scalar(
            nmr[:], mvst[:, 2:3], rstdc[:, 0:1], -1.0, ALU.mult, ALU.mult),
            deps=(t_rsd,))
        t_nm = sched.op("act", lambda e, oslot=oslot: e.activation(
            oslot, oslot, AF.Identity, bias=nmr[:, 0:1], scale=rstdc[:, 0:1]),
            deps=(t_nm0, t_sq))
        t_ostores[t] = sched.dma("sp", out_d[t * 128:(t + 1) * 128, :],
                                 oslot, deps=(t_nm,))
        ln2_prev = t_nm

    finals = sched.all_dma_barrier("sp")
    sched.finalize()
    with es, nc.Block() as block:
        @block.sync
        def _(sync):
            sched.replay("sp", sync)
            for (sname, v) in finals:
                sync.wait_ge(sched.sems[sname], v)

        @block.vector
        def _(v):
            sched.replay("dve", v)

        @block.scalar
        def _(s):
            sched.replay("act", s)

        @block.gpsimd
        def _(g):
            sched.replay("pool", g)

        @block.tensor
        def _(t):
            sched.replay("pe", t)
    return nc


_NC = None


def kernel(L0_attn_input_data, L0_attn_W1, L0_attn_W2, L0_attn_bias,
           L0_attn_embeddings_table, L0_ffn_w_gate, L0_ffn_w_up,
           L0_ffn_w_down):
    global _NC
    x = np.ascontiguousarray(np.asarray(L0_attn_input_data, dtype=np.float32))
    W1 = np.ascontiguousarray(np.asarray(L0_attn_W1, dtype=np.float32))
    W2 = np.ascontiguousarray(np.asarray(L0_attn_W2, dtype=np.float32))
    bias = np.ascontiguousarray(
        np.asarray(L0_attn_bias, dtype=np.float32)).reshape(1, D)
    et = np.ascontiguousarray(
        np.asarray(L0_attn_embeddings_table, dtype=np.float32))
    wg = np.ascontiguousarray(np.asarray(L0_ffn_w_gate, dtype=np.float32))
    wu = np.ascontiguousarray(np.asarray(L0_ffn_w_up, dtype=np.float32))
    wd = np.ascontiguousarray(np.asarray(L0_ffn_w_down, dtype=np.float32))

    if _NC is None:
        _NC = build_fused()
    ident = np.eye(21, dtype=np.float32)
    eye128 = np.eye(128, dtype=np.float32)
    ins = [{"x": x[c], "w1": W1, "w2": W2, "bias": bias, "et": et,
            "ident": ident, "eye128": eye128,
            "wg": wg, "wu": wu, "wd": wd} for c in range(NCORES)]
    res = run_bass_kernel_spmd(_NC, ins, core_ids=list(range(NCORES)))
    out = np.stack([res.results[c]["out"] for c in range(NCORES)], axis=0)
    return out.astype(np.float32)


# version 20 (final)
# speedup vs baseline: 1.0793x; 1.0195x over previous
import numpy as np
import concourse.bass as bass
import concourse.mybir as mybir
from concourse.bass_utils import run_bass_kernel_spmd
from contextlib import ExitStack

F32 = mybir.dt.float32
BF16 = mybir.dt.bfloat16
FP16 = mybir.dt.float16
ALU = mybir.AluOpType
AF = mybir.ActivationFunctionType
AX = mybir.AxisListType

NCORES = 8
N, D, I = 2048, 1024, 2816
NT = N // 128
KC = D // 128
IC = I // 128
RND_HC = 6291456.0
VELT_C = 4194305.0
PE_TAPS = list(range(1, 21))
PAD = 32
QW = 1024


class Sched:

    ENGS = ("pe", "dve", "act", "pool", "sp")
    NPOOL = {"act": 12, "pool": 8, "sp": 20}

    def __init__(self, nc):
        self.nc = nc
        self.items = {e: [] for e in self.ENGS}
        self.count = {}
        self.sems = {}
        self.rot = {e: 0 for e in self.NPOOL}

    def make_sems(self, es):
        for e in self.ENGS:
            self.sems[e] = es.enter_context(self.nc.semaphore(f"sem_{e}"))
            self.count[e] = 0
        for e, np_ in self.NPOOL.items():
            for i in range(np_):
                q = f"{e}q{i}"
                self.sems[q] = es.enter_context(self.nc.semaphore(f"sem_{q}"))
                self.count[q] = 0

    def op(self, eng, fn, deps=()):
        self.count[eng] += 1
        tok = (eng, self.count[eng])
        self.items[eng].append(
            (fn, tuple(d for d in deps if d is not None), tok, 1, eng))
        return tok

    def dma(self, eng, out, in_, deps=(), **kw):
        assert eng in self.NPOOL
        q = f"{eng}q{self.rot[eng]}"
        self.rot[eng] = (self.rot[eng] + 1) % self.NPOOL[eng]
        prev = self.count[q]
        deps = tuple(d for d in deps if d is not None)
        if prev > 0:
            deps = deps + ((q, prev),)
        self.count[q] += 16
        tok = (q, self.count[q])
        self.items[eng].append(
            (lambda e: e.dma_start(out=out, in_=in_, **kw), deps, tok, 16, q))
        return tok

    def finalize(self):
        self.prior = {}
        for eng in self.ENGS:
            run = {}
            for (fn, deps, tok, inc, sem) in self.items[eng]:
                self.prior[tok] = dict(run)
                run[sem] = tok[1]

    def replay(self, eng, e):
        waited = {}
        for (fn, deps, tok, inc, sem) in self.items[eng]:
            want = {}
            for (s, v) in deps:
                if s == eng:
                    v = min(v, self.prior[tok].get(s, 0))
                if v > want.get(s, 0):
                    want[s] = v
            for s, v in want.items():
                if v > waited.get(s, 0):
                    e.wait_ge(self.sems[s], v)
                    waited[s] = v
            inst = fn(e)
            if inst is not None:
                inst.then_inc(self.sems[sem], inc)

    def all_dma_barrier(self, eng):
        return tuple((f"{eng}q{i}", self.count[f"{eng}q{i}"])
                     for i in range(self.NPOOL[eng])
                     if self.count[f"{eng}q{i}"] > 0)


class Arena:
    def __init__(self, nc, total_bytes):
        self.nc = nc
        self.total = total_bytes
        self.handle = None
        self.named = []

    def reserve(self, es):
        self.handle = es.enter_context(
            self.nc.sbuf_tensor("arena", [128, self.total], mybir.dt.uint8))

    def at(self, name, offset, shape, dtype):
        per_part = int(np.prod(shape[1:])) * mybir.dt.size(dtype)
        assert offset % 32 == 0, (name, offset)
        assert offset + per_part <= self.total, (name, offset, per_part)
        h = self.nc.virtual_sbuf_tensor(name, shape, dtype)
        self.named.append((h, offset))
        return h

    def finalize(self):
        base = self.nc.lookup_mloc(self.handle).addr
        for h, off in self.named:
            self.nc._tensor(h.name, h.shape, h.dtype, type="SB")
            mloc = self.nc.lookup_mloc(h)
            mloc.allocated = not self.nc.target_bir_lowering
            mloc.addr = base + off


MAP_P = {"u": "pool", "w": "pool", "hi": "pool", "G1": "pool", "q": "pool"}
MAP_D = {"u": "pool", "w": "dve", "hi": "dve", "G1": "pool", "q": "dve"}


def qv_stages(sched, bufs, src_ap, q_out_ap, width, src_tok,
              extra_q_deps, prev1, prev2, emap, T):
    nb = width // 16
    amax, s_t, rs_t = bufs["amax"], bufs["s"], bufs["rs"]
    u, cu, w = bufs["u"], bufs["cu"], bufs["w"]
    hi, Ra, Rb, lcp = bufs["hi"], bufs["Ra"], bufs["Rb"], bufs["lcp"]
    p1 = prev1 or {}
    p2 = prev2 or {}

    def g(d, *keys):
        return tuple(d[k] for k in keys if d.get(k) is not None)

    src_deps = tuple(src_tok) if isinstance(src_tok, (tuple, list)) and \
        src_tok and isinstance(src_tok[0], tuple) else (src_tok,)
    src3 = src_ap.rearrange("p (b k) -> p b k", k=16)
    rs_b = rs_t[:, :nb].unsqueeze(2).broadcast_to([128, nb, 16])
    s_b = s_t[:, :nb].unsqueeze(2).broadcast_to([128, nb, 16])
    u3 = u[:, :width].rearrange("p (b k) -> p b k", k=16)
    uf, cuf, wf = u[:, :width], cu[:, :width], w[:, :width]
    hif, raf, rbf, lcpf = (hi[:, :width], Ra[:, :width], Rb[:, :width],
                           lcp[:, :width])
    ctok = tuple(bufs.get("ctok") or ())
    cC = bufs["cC"][:, 0:1]
    cH = bufs["cH"][:, 0:1]
    cn1 = bufs["cn1"][:, 0:1]
    q3 = q_out_ap.rearrange("p (b k) -> p b k", k=16)
    hi3 = hif.rearrange("p (b k) -> p b k", k=16)

    def st_am():
        T["am"] = sched.op("dve", lambda e: e.tensor_reduce(
            amax[:, :nb], src3, AX.X, ALU.max, apply_absolute_value=True),
            deps=src_deps + g(p2, "s"))

    def st_s():
        T["s"] = sched.op("dve", lambda e: e.tensor_scalar(
            s_t[:, :nb], amax[:, :nb], 1.0 / 6.0, 1e-12, ALU.mult, ALU.max),
            deps=(T["am"],) + g(p2, "q", "rs"))

    def st_rs():
        T["rs"] = sched.op("dve", lambda e: e.reciprocal(
            rs_t[:, :nb], s_t[:, :nb]), deps=(T["s"],) + g(p2, "u"))

    def st_u():
        T["u"] = sched.op(emap["u"], lambda e: e.tensor_tensor(
            u3, src3, rs_b, ALU.mult), deps=(T["rs"],) + g(p1, "b5p"))

    def st_cu():
        T["cu"] = sched.op("act", lambda e: e.activation(
            cuf, uf, AF.Identity, scale=cC),
            deps=(T["u"],) + g(p1, "lcp") + ctok)

    def st_w():
        T["w"] = sched.op(emap["w"], lambda e: e.tensor_tensor(
            wf, cuf, uf, ALU.subtract), deps=(T["cu"],) + g(p1, "G"))

    def st_hi():
        T["hi"] = sched.op(emap["hi"], lambda e: e.tensor_tensor(
            hif, cuf, wf, ALU.subtract), deps=(T["w"],) + g(p1, "q"))

    def st_b5p():
        T["b5p"] = sched.op("act", lambda e: e.activation(
            cuf, uf, AF.Identity, bias=cH), deps=(T["hi"],))

    def st_Ra():
        T["Ra"] = sched.op("act", lambda e: e.activation(
            raf, hif, AF.Relu, bias=cn1), deps=(T["hi"],))

    def st_Rb():
        T["Rb"] = sched.op("act", lambda e: e.activation(
            rbf, hif, AF.Relu, bias=cn1, scale=cn1), deps=(T["Ra"],))

    def st_lcp():
        T["lcp"] = sched.op("dve", lambda e: e.tensor_scalar(
            lcpf, cuf, RND_HC, -1.0, ALU.subtract, ALU.max),
            deps=(T["b5p"],))

    def st_G1():
        T["G1"] = sched.op(emap["G1"], lambda e: e.tensor_tensor(
            raf, raf, rbf, ALU.subtract), deps=(T["Rb"],))

    def st_G():
        T["G"] = sched.op("dve", lambda e: e.scalar_tensor_tensor(
            hif, lcpf, 1.0, raf, ALU.min, ALU.add),
            deps=(T["lcp"], T["G1"]))

    def st_q():
        T["q"] = sched.op(emap["q"], lambda e: e.tensor_tensor(
            q3, hi3, s_b, ALU.mult), deps=(T["G"],) + tuple(extra_q_deps))

    return [st_am, st_s, st_rs, st_u, st_cu, st_w, st_hi, st_b5p,
            st_Ra, st_Rb, st_lcp, st_G1, st_G, st_q]


def emit_quant_v(sched, bufs, src_ap, q_out_ap, width, src_tok,
                 extra_q_deps=(), prev1=None, prev2=None, q_on_dve=False,
                 emap=None):
    T = {}
    emap = emap or (dict(MAP_P, q="dve") if q_on_dve else MAP_P)
    for st in qv_stages(sched, bufs, src_ap, q_out_ap, width, src_tok,
                        extra_q_deps, prev1, prev2, emap, T):
        st()
    return T


def emit_quant_pair(sched, specs):
    Ts = [dict() for _ in specs]
    plans = [qv_stages(sched, b, s, q, wd, tok, ex, p1, p1, em, Ts[i])
             for i, (b, s, q, wd, tok, ex, p1, em) in enumerate(specs)]
    for si in range(len(plans[0])):
        for p in plans:
            p[si]()
    return Ts


def build_fused():
    nc = bass.Bass("TRN2", target_bir_lowering=False, debug=False)
    x_d = nc.dram_tensor("x", [N, D], F32, kind="ExternalInput").ap()
    w1_d = nc.dram_tensor("w1", [D, D], F32, kind="ExternalInput").ap()
    w2_d = nc.dram_tensor("w2", [D, D], F32, kind="ExternalInput").ap()
    bias_d = nc.dram_tensor("bias", [1, D], F32, kind="ExternalInput").ap()
    et_d = nc.dram_tensor("et", [21, D], F32, kind="ExternalInput").ap()
    id_d = nc.dram_tensor("ident", [21, 21], F32, kind="ExternalInput").ap()
    eye_d = nc.dram_tensor("eye128", [128, 128], F32, kind="ExternalInput").ap()
    wg_d = nc.dram_tensor("wg", [I, D], F32, kind="ExternalInput").ap()
    wu_d = nc.dram_tensor("wu", [I, D], F32, kind="ExternalInput").ap()
    wd_d = nc.dram_tensor("wd", [D, I], F32, kind="ExternalInput").ap()
    out_d = nc.dram_tensor("out", [N, D], F32, kind="ExternalOutput").ap()

    xb_dram = nc.dram_tensor("xb_dram", [N, D], FP16, kind="Internal").ap()
    y1_dram = nc.dram_tensor("y1_dram", [N, D], F32, kind="Internal").ap()
    y1q_dram = nc.dram_tensor("y1q_dram", [N, D], BF16, kind="Internal").ap()
    hq_dram = nc.dram_tensor("hq_dram", [N, I], BF16, kind="Internal").ap()
    wgq_d = nc.dram_tensor("wgq", [I, D], BF16, kind="Internal").ap()
    wuq_d = nc.dram_tensor("wuq", [I, D], BF16, kind="Internal").ap()
    wdq_d = nc.dram_tensor("wdq", [D, I], BF16, kind="Internal").ap()

    es = ExitStack()
    sched = Sched(nc)
    sched.make_sems(es)

    AR = 207 * 1024
    ar = Arena(nc, AR)
    ar.reserve(es)

    y1qT = ar.at("y1qT", 0, [128, KC * N], BF16)
    convA = ar.at("convA", 0, [128, N], FP16)
    convB = ar.at("convB", 4096, [128, N], FP16)
    convC = ar.at("convC", 8192, [128, N], FP16)
    convD = ar.at("convD", 12288, [128, N], FP16)
    convE = ar.at("convE", 16384, [128, N], FP16)
    et_rows = ar.at("et_rows", 20480, [21, D], F32)
    ident21 = ar.at("ident21", 24576, [21, 21], F32)

    GEN = 32768
    x_res = ar.at("x_res", GEN, [128, NT * D], F32)
    xT = ar.at("xT", GEN + 65536, [128, KC * (N + PAD)], FP16)
    hT = ar.at("hT", GEN + 98816, [128, KC * N], FP16)
    W1b = ar.at("W1b", GEN + 131584, [128, KC * D], FP16)
    W2b = ar.at("W2b", GEN + 148096, [128, KC * D], FP16)

    def mk_qset(tag, W, u_o, cu_o, w_o, hi_o, ra_o, rb_o, am_o, s_o, rs_o,
                src_o=None, q_o=None):
        d = {"u": ar.at(f"{tag}_u", u_o, [128, W], F32),
             "cu": ar.at(f"{tag}_cu", cu_o, [128, W], F32),
             "w": ar.at(f"{tag}_w", w_o, [128, W], F32),
             "lcp": ar.at(f"{tag}_lcp", w_o, [128, W], BF16),
             "hi": ar.at(f"{tag}_hi", hi_o, [128, W], BF16),
             "Ra": ar.at(f"{tag}_Ra", ra_o, [128, W], BF16),
             "Rb": ar.at(f"{tag}_Rb", rb_o, [128, W], BF16),
             "amax": ar.at(f"{tag}_am", am_o, [128, W // 16], F32),
             "s": ar.at(f"{tag}_s", s_o, [128, W // 16], F32),
             "rs": ar.at(f"{tag}_rs", rs_o, [128, W // 16], F32)}
        if src_o is not None:
            d["src"] = ar.at(f"{tag}_src", src_o, [128, W], F32)
        if q_o is not None:
            d["qout"] = ar.at(f"{tag}_q", q_o, [128, W], BF16)
        return d

    vw2 = [mk_qset("vw0", 1024, 49152, 53248, 57344, 61440, 63488, 65536,
                   69632, 69888, 70144, src_o=45056, q_o=67584),
           mk_qset("vw1", 1024, 74496, 78592, 82688, 86784, 88832, 90880,
                   94976, 95232, 95488, src_o=70400, q_o=92928)]

    vy1 = {
        "u": ar.at("vy_u", 197248, [128, QW], F32),
        "cu": ar.at("vy_cu", 201344, [128, QW], F32),
        "w": ar.at("vy_w", 16384, [128, QW], F32),
        "lcp": ar.at("vy_lcp", 16384, [128, QW], BF16),
        "hi": ar.at("vy_hi", 2048, [128, QW], BF16),
        "Ra": ar.at("vy_Ra", 6144, [128, QW], BF16),
        "Rb": ar.at("vy_Rb", 10240, [128, QW], BF16),
        "amax": ar.at("vy_amax", 205440, [128, QW // 16], F32),
        "s": ar.at("vy_s", 205728, [128, QW // 16], F32),
        "rs": ar.at("vy_rs", 206016, [128, QW // 16], F32),
    }
    y1q_st = [ar.at("y1qs0", 26624, [128, D], BF16),
              ar.at("y1qs1", 30720, [128, D], BF16)]
    vy1_lcp = [ar.at("vy_lcp0", 16384, [128, 512], BF16),
               ar.at("vy_lcp1", 18432, [128, 512], BF16)]
    diag16 = ar.at("diag16", 197248, [128, len(PE_TAPS) * 128], FP16)

    wgT = ar.at("wgT", 32768, [128, KC * I], BF16)
    wuT = ar.at("wuT", 77824, [128, KC * I], BF16)
    vh2 = [mk_qset("vh0", 1408, 122880, 134144, 151040, 145408, 175776,
                   181408, 162304, 163040, 163776),
           mk_qset("vh1", 1408, 128512, 139776, 156672, 148224, 178592,
                   184224, 162656, 163392, 164128)]
    h_t = ar.at("h_t", 164512, [128, I], F32)
    hq_t = ar.at("hq_t", 187040, [128, I], BF16)
    wd_src = ar.at("wd_src", 192672, [128, 1408], F32)
    wd_src2 = ar.at("wd_src2", 198304, [128, 1408], F32)
    sg_b = [None] * 4
    up_b = [None] * 4
    cur2 = 199456
    for i in range(4):
        sg_b[i] = ar.at(f"sg{i}", cur2, [128, 512], BF16); cur2 += 1024
        up_b[i] = ar.at(f"up{i}", cur2, [128, 512], BF16); cur2 += 1024
    assert cur2 <= 208160, cur2

    hqT = ar.at("hqT", 32768, [128, IC * N], BF16)
    wdT = ar.at("wdT", 122880, [128, IC * D], BF16)
    y1r = ar.at("y1r", 167936, [128, 3 * D], F32)
    outb = ar.at("outb", 180224, [128, 3 * D], F32)

    sm = 208160
    et_fm = ar.at("et_fm", sm, [128, KC * 21], F32); sm += KC * 21 * 4
    sm = (sm + 31) // 32 * 32
    biasb = ar.at("biasb", sm, [1, D], BF16); sm += D * 2
    onesb = ar.at("onesb", sm, [1, 128], BF16); sm += 256
    epsc = ar.at("epsc", sm, [128, 1], F32); sm += 32
    mvst = ar.at("mvst", sm, [128, 8], F32); sm += 32
    bnbuf = ar.at("bnbuf", sm, [128, 12], F32); sm += 64
    rstdc = ar.at("rstdc", sm, [128, 1], F32); sm += 32
    sqc = ar.at("sqc", sm, [128, 1], F32); sm += 32
    nmr = ar.at("nmr", sm, [128, 1], F32); sm += 32
    eye16 = ar.at("eye16", sm, [128, 128], FP16); sm += 256
    cvelt = ar.at("cvelt", sm, [128, 1], F32); sm += 32
    crndhc = ar.at("crndhc", sm, [128, 1], F32); sm += 32
    cneg1 = ar.at("cneg1", sm, [128, 1], F32); sm += 32
    assert sm <= AR, sm

    att_ps = [es.enter_context(nc.psum_tensor(f"aps{i}", [128, 1024], F32))
              for i in range(4)]

    ar.finalize()

    t_ep = sched.op("dve", lambda e: e.memset(epsc[:], 1e-5))
    t_ones = sched.op("dve", lambda e: e.memset(onesb[:], 1.0))
    ctok = (sched.op("dve", lambda e: e.memset(cvelt[:], VELT_C)),
            sched.op("dve", lambda e: e.memset(crndhc[:], RND_HC)),
            sched.op("dve", lambda e: e.memset(cneg1[:], -1.0)))
    consts = {"cC": cvelt, "cH": crndhc, "cn1": cneg1, "ctok": ctok}
    for b in vw2 + vh2 + [vy1]:
        b.update(consts)
    vy1h = []
    for o in range(2):
        hset = {k: vy1[k][:, o * 512:(o + 1) * 512]
                for k in ("u", "cu", "w", "hi", "Ra", "Rb")}
        for k in ("amax", "s", "rs"):
            hset[k] = vy1[k][:, o * 32:(o + 1) * 32]
        hset["lcp"] = vy1_lcp[o]
        hset.update(consts)
        vy1h.append(hset)
    t_pad = [sched.op("dve", lambda e, k=k: e.memset(
        xT[:, k * (N + PAD): k * (N + PAD) + PAD], 0.0)) for k in range(KC)]

    t_xres = [None] * NT
    for t in range(3):
        t_xres[t] = sched.dma("sp", x_res[:, t * D:(t + 1) * D],
                              x_d[t * 128:(t + 1) * 128, :])

    wjobs = []
    for src_d, dst_d in ((wg_d, wgq_d), (wu_d, wuq_d)):
        for j in range(IC):
            wjobs.append((src_d[j * 128:(j + 1) * 128, :],
                          dst_d[j * 128:(j + 1) * 128, :]))
    NWJ = len(wjobs)
    wload = [None] * NWJ
    wstore = [None] * NWJ
    wq_prev = [None, None]
    wload[0] = sched.dma("sp", vw2[0]["src"][:, :], wjobs[0][0])
    wload[1] = sched.dma("sp", vw2[1]["src"][:, :], wjobs[1][0])

    def emit_wpair(j):
        specs = []
        for o, em in ((0, MAP_P), (1, MAP_D)):
            p = (j + o) % 2
            S = vw2[p]
            ex = (wstore[j + o - 2],) if j + o >= 2 else ()
            specs.append((S, S["src"][:, :], S["qout"][:, :], 1024,
                          (wload[j + o],), ex, wq_prev[p], em))
        Ts = emit_quant_pair(sched, specs)
        for o in (0, 1):
            p = (j + o) % 2
            wq_prev[p] = Ts[o]
            wstore[j + o] = sched.dma("sp", wjobs[j + o][1],
                                      vw2[p]["qout"][:, :],
                                      deps=(Ts[o]["q"],))
            if j + o + 2 < NWJ:
                wload[j + o + 2] = sched.dma("sp", vw2[p]["src"][:, :],
                                             wjobs[j + o + 2][0],
                                             deps=(Ts[o]["u"],))

    t_xb = sched.dma("pool", xb_dram[:], x_d[:])
    t_xT = [sched.dma("sp", xT[:, k * (N + PAD) + PAD:(k + 1) * (N + PAD)],
                      xb_dram[:, k * 128:(k + 1) * 128],
                      deps=(t_xb, t_pad[k]), transpose=True) for k in range(KC)]
    t_w1 = [sched.dma("pool", W1b[:, k * D:(k + 1) * D],
                      w1_d[k * 128:(k + 1) * 128, :]) for k in range(KC)]
    t_w2 = [sched.dma("pool", W2b[:, k * D:(k + 1) * D],
                      w2_d[k * 128:(k + 1) * 128, :]) for k in range(KC)]
    t_bias = sched.dma("pool", biasb[:], bias_d[:])
    t_eye = sched.dma("pool", eye16[:], eye_d[:])
    t_etl = sched.dma("sp", et_rows[:], et_d[:])
    t_id = [sched.dma("sp", ident21[:], id_d[:])]
    t_etfm = []
    for k in range(KC):
        t_tr = sched.op("pe", lambda e, k=k: e.transpose(
            att_ps[0][:, 0:21], et_rows[0:21, k * 128:(k + 1) * 128], ident21[:]),
            deps=tuple(t_id) + (t_etl, t_etfm[-1] if t_etfm else None))
        t_cp = sched.op("dve", lambda e, k=k: e.tensor_copy(
            et_fm[:, k * 21:(k + 1) * 21], att_ps[0][:, 0:21]), deps=(t_tr,))
        t_etfm.append(t_cp)
    t_et_done = t_etfm[-1]

    def xsh(k, j):
        base = k * (N + PAD) + PAD - j
        return xT[:, base: base + N]

    def etc(k, j):
        return et_fm[:, k * 21 + j: k * 21 + j + 1]

    conv_done = []
    conv_ps = [att_ps[3][:, 0:512], att_ps[3][:, 512:1024]]
    cp_guard = [None, None]
    cpar = [0]
    wj_next = [0]
    for k in range(KC):
        npe = len(PE_TAPS)
        eye_b = eye16[:].unsqueeze(1).broadcast_to([128, npe, 128])
        etv = et_fm[:, k * 21 + PE_TAPS[0]: k * 21 + PE_TAPS[0] + npe]
        et_b = etv.unsqueeze(2).broadcast_to([128, npe, 128])
        t_dg = sched.op("dve", lambda e, k=k, eye_b=eye_b, et_b=et_b: e.tensor_tensor(
            diag16[:].rearrange("p (j m) -> p j m", m=128), eye_b, et_b,
            ALU.mult), deps=(t_et_done, t_eye) +
            (tuple(conv_done[-1]) if conv_done else ()))
        pe_evacs = []
        for nt in range(4):
            par = cpar[0] % 2
            cpar[0] += 1

            def mk_conv(e, k=k, nt=nt, par=par):
                last = None
                for ji, j in enumerate(PE_TAPS):
                    e.ldweights(diag16[:, ji * 128:(ji + 1) * 128])
                    last = e.matmul(
                        conv_ps[par], diag16[:, ji * 128:(ji + 1) * 128],
                        xT[:, k * (N + PAD) + PAD - j + nt * 512:
                           k * (N + PAD) + PAD - j + nt * 512 + 512],
                        start=(ji == 0), stop=(ji == len(PE_TAPS) - 1))
                return last
            cdeps = (t_dg, t_xT[k])
            if cp_guard[par] is not None:
                cdeps = cdeps + (cp_guard[par],)
            t_cmm = sched.op("pe", mk_conv, deps=cdeps)
            t_evc = sched.op("act", lambda e, k=k, nt=nt, par=par: e.activation(
                hT[:, k * N + nt * 512: k * N + (nt + 1) * 512],
                conv_ps[par], AF.Identity), deps=(t_cmm,))
            cp_guard[par] = t_evc
            pe_evacs.append(t_evc)
        conv_done.append(tuple(pe_evacs))
        for _ in range(3):
            if wj_next[0] < NWJ:
                emit_wpair(wj_next[0])
                wj_next[0] += 2
    conv_done = [t for evs in conv_done for t in evs]
    assert wj_next[0] == NWJ
    for t in range(3, NT):
        t_xres[t] = sched.dma("sp", x_res[:, t * D:(t + 1) * D],
                              x_d[t * 128:(t + 1) * 128, :],
                              deps=(wstore[NWJ - 1], wstore[NWJ - 2]))

    def mk_mm(e, t, ps):
        last = None
        for k in range(KC):
            stat = xT[:, k * (N + PAD) + PAD + t * 128:
                      k * (N + PAD) + PAD + (t + 1) * 128]
            e.ldweights(stat)
            e.matmul(ps[:, 0:512], stat, W1b[:, k * D: k * D + 512],
                     start=(k == 0), stop=False)
            last = e.matmul(ps[:, 512:1024], stat,
                            W1b[:, k * D + 512:(k + 1) * D],
                            start=(k == 0), stop=False)
        return last

    def mk_mm2(e, t, ps):
        for k in range(KC):
            stat = hT[:, k * N + t * 128: k * N + (t + 1) * 128]
            e.ldweights(stat)
            e.matmul(ps[:, 0:512], stat, W2b[:, k * D: k * D + 512],
                     start=False, stop=False)
            e.matmul(ps[:, 512:1024], stat, W2b[:, k * D + 512:(k + 1) * D],
                     start=False, stop=False)
        e.ldweights(onesb[0:1, :])
        e.matmul(ps[:, 0:512], onesb[0:1, :], biasb[0:1, 0:512],
                 start=False, stop=True)
        return e.matmul(ps[:, 512:1024], onesb[0:1, :], biasb[0:1, 512:1024],
                        start=False, stop=True)

    attn_evac = [None] * NT
    tok_x = [None] * NT
    t_y1d = [None] * NT
    ln_prev = None
    y1q_tok = [None] * NT
    y1q_dump = [None] * NT
    qprev = {}
    prev_dump = {0: None, 1: None}

    for t in range(3):
        tok_x[t] = sched.op("pe", lambda e, t=t: mk_mm(e, t, att_ps[t % 3]),
                            deps=tuple(t_xT) + tuple(t_w1) + (t_et_done,))
    for t in range(NT):
        ps = att_ps[t % 3]
        tok_h = sched.op("pe", lambda e, t=t, ps=ps: mk_mm2(e, t, ps),
                         deps=tuple(conv_done) + tuple(t_w2) +
                         (tok_x[t], t_ones, t_bias))
        t_ev = sched.op("dve", lambda e, t=t, ps=ps: e.tensor_tensor(
            x_res[:, t * D:(t + 1) * D], ps[:, :], x_res[:, t * D:(t + 1) * D],
            ALU.add), deps=(tok_h, t_xres[t]))
        attn_evac[t] = t_ev
        if t + 3 < NT:
            tok_x[t + 3] = sched.op(
                "pe", lambda e, t=t: mk_mm(e, t + 3, att_ps[(t + 3) % 3]),
                deps=(t_ev,))
        t_bn0 = sched.op("dve", lambda e, t=t: e.bn_stats(
            bnbuf[:, 0:6], x_res[:, t * D: t * D + 512]), deps=(t_ev, ln_prev))
        t_bn = sched.op("dve", lambda e, t=t: e.bn_stats(
            bnbuf[:, 6:12], x_res[:, t * D + 512:(t + 1) * D]), deps=(t_bn0,))
        t_ag = sched.op("dve", lambda e: e.bn_aggr(mvst[:, 0:2], bnbuf[:]),
                        deps=(t_bn,))
        t_sq = sched.op("act", lambda e: e.activation(
            sqc[:], mvst[:, 1:2], AF.Sqrt, bias=epsc[:, 0:1]), deps=(t_ag, t_ep))
        t_rsd = sched.op("dve", lambda e: e.reciprocal(rstdc[:], sqc[:]),
                         deps=(t_sq,))
        t_nm0 = sched.op("dve", lambda e: e.tensor_scalar(
            nmr[:], mvst[:, 0:1], rstdc[:, 0:1], -1.0, ALU.mult, ALU.mult),
            deps=(t_rsd,))
        t_nm = sched.op("act", lambda e, t=t: e.activation(
            x_res[:, t * D:(t + 1) * D], x_res[:, t * D:(t + 1) * D],
            AF.Identity, bias=nmr[:, 0:1], scale=rstdc[:, 0:1]),
            deps=(t_nm0, t_sq))
        ln_prev = t_nm
        t_y1d[t] = sched.dma("sp", y1_dram[t * 128:(t + 1) * 128, :],
                             x_res[:, t * D:(t + 1) * D], deps=(t_nm,))
        stg = y1q_st[t % 2]
        qdeps = ((prev_dump[t % 2],) if prev_dump[t % 2] else ())
        specs = []
        for o, em in ((0, MAP_P), (1, MAP_D)):
            c0 = o * 512
            specs.append((vy1h[o], x_res[:, t * D + c0: t * D + c0 + 512],
                          stg[:, c0:c0 + 512], 512, (t_nm,), qdeps,
                          qprev.get(o), em))
        Ts = emit_quant_pair(sched, specs)
        qprev[0], qprev[1] = Ts
        y1q_tok[t] = (Ts[0]["q"], Ts[1]["q"])
        y1q_dump[t] = sched.dma("sp", y1q_dram[t * 128:(t + 1) * 128, :],
                                stg[:], deps=y1q_tok[t])
        prev_dump[t % 2] = y1q_dump[t]

    y1qT_half = [None, None]
    for half in range(2):
        lst = []
        for k in range(KC):
            dd = tuple(y1q_dump[half * 8:(half + 1) * 8])
            if (half == 0 and k == 4) or (half == 1 and k in (0, 1, 2, 4)):
                dd = dd + y1q_tok[NT - 1]
            lst.append(sched.dma(
                "sp", y1qT[:, k * N + half * 1024: k * N + half * 1024 + 1024],
                y1q_dram[half * 1024:(half + 1) * 1024, k * 128:(k + 1) * 128],
                deps=dd, transpose=True))
        y1qT_half[half] = lst

    def xdead(byte0, byte1):
        t0 = max(0, (byte0 - 32768) // 4096)
        t1 = min(NT - 1, (byte1 - 1 - 32768) // 4096)
        dd = []
        for tt in range(t0, t1 + 1):
            dd += [*y1q_tok[tt], t_y1d[tt]]
        return tuple(dd)
    wsts = tuple(wstore)
    t_wgT = [sched.dma("sp", wgT[:, k * I:(k + 1) * I],
                       wgq_d[:, k * 128:(k + 1) * 128], transpose=True,
                       deps=xdead(32768 + k * 5632, 32768 + (k + 1) * 5632) +
                       wsts)
             for k in range(KC)]
    t_wuT = [sched.dma("sp", wuT[:, k * I:(k + 1) * I],
                       wuq_d[:, k * 128:(k + 1) * 128], transpose=True,
                       deps=xdead(77824 + k * 5632, 77824 + (k + 1) * 5632) +
                       ((attn_evac[NT - 1],) if k >= 4 else ()) + wsts)
             for k in range(KC)]

    IBANKS = [(ib * 512, min(512, I - ib * 512)) for ib in range((I + 511) // 512)]
    t_hq_store = [None] * NT
    wd_store = [None] * 16
    psum_guard = {}
    h_guard = [()]
    range_prev = [None, None]
    chain2 = [None, None]
    tog = [0]
    h_parts_of = [None] * NT
    pslot = [0]

    def emit_ffn_tile(t):
        h_parts = []
        for bi, (i0, w) in enumerate(IBANKS):
            ps = att_ps[pslot[0] % 4]
            sg = sg_b[pslot[0] % 4]
            uv = up_b[pslot[0] % 4]
            deps = tuple(t_wgT) + tuple(t_wuT) + tuple(y1qT_half[t // 8])
            if psum_guard.get(pslot[0] % 4) is not None:
                deps = deps + (psum_guard[pslot[0] % 4],)
            pslot[0] += 1

            def mk_gmm(e, t=t, i0=i0, w=w, ps=ps):
                last = None
                for k in range(KC):
                    stat = y1qT[:, k * N + t * 128: k * N + (t + 1) * 128]
                    e.ldweights(stat)
                    e.matmul(ps[:, 0:w], stat, wgT[:, k * I + i0: k * I + i0 + w],
                             start=(k == 0), stop=(k == KC - 1))
                    last = e.matmul(ps[:, 512:512 + w], stat,
                                    wuT[:, k * I + i0: k * I + i0 + w],
                                    start=(k == 0), stop=(k == KC - 1))
                return last
            t_mm = sched.op("pe", mk_gmm, deps=deps)
            t_sg = sched.op("act", lambda e, sg=sg, ps=ps, w=w: e.activation(
                sg[:, 0:w], ps[:, 0:w], AF.Silu), deps=(t_mm,))
            t_uv = sched.op("act", lambda e, uv=uv, ps=ps, w=w: e.activation(
                uv[:, 0:w], ps[:, 512:512 + w], AF.Identity), deps=(t_sg,))
            t_hm = sched.op("dve",
                            lambda e, sg=sg, uv=uv, i0=i0, w=w:
                            e.tensor_tensor(h_t[:, i0:i0 + w], sg[:, 0:w],
                                            uv[:, 0:w], ALU.mult),
                            deps=(t_uv,) + h_guard[0])
            psum_guard[(pslot[0] - 1) % 4] = t_hm
            h_parts.append(t_hm)
        h_parts_of[t] = h_parts

    def emit_hquant(t):
        specs = []
        ps = []
        for half, em in ((0, MAP_P), (1, MAP_D)):
            p = tog[0] % 2
            tog[0] += 1
            ps.append(p)
            S = vh2[p]
            c0 = half * 1408
            ex = (range_prev[half],) if range_prev[half] else ()
            specs.append((S, h_t[:, c0:c0 + 1408], hq_t[:, c0:c0 + 1408],
                          1408, tuple(h_parts_of[t]), ex, chain2[p], em))
        Ts = emit_quant_pair(sched, specs)
        for half in range(2):
            chain2[ps[half]] = Ts[half]
        h_guard[0] = (Ts[0]["u"], Ts[1]["u"])
        t_st = sched.dma("sp", hq_dram[t * 128:(t + 1) * 128, :],
                         hq_t[0:128, :], deps=(Ts[0]["q"], Ts[1]["q"]))
        t_hq_store[t] = t_st
        range_prev[0] = t_st
        range_prev[1] = t_st

    wd_jobs = []
    for i in range(8):
        for c in (0, 1408):
            wd_jobs.append((wd_d[i * 128:(i + 1) * 128, c:c + 1408],
                            wdq_d[i * 128:(i + 1) * 128, c:c + 1408]))
    wd_load = [None] * 16
    wd_T = [None] * 16

    def emit_wdpair(i):
        specs = []
        ps = []
        for o, em in ((0, MAP_P), (1, MAP_D)):
            prevT = wd_T[i + o - 2] if i + o - 2 >= 0 else None
            ld = (prevT["u"],) if prevT else y1q_tok[NT - 1]
            wd_load[i + o] = sched.dma("sp", (wd_src, wd_src2)[o][:, :],
                                       wd_jobs[i + o][0], deps=ld)
            p = tog[0] % 2
            tog[0] += 1
            ps.append(p)
            S = vh2[p]
            c0 = o * 1408
            ex = (range_prev[o],) if range_prev[o] else ()
            specs.append((S, (wd_src, wd_src2)[o][:, :],
                          hq_t[:, c0:c0 + 1408], 1408, (wd_load[i + o],),
                          ex, chain2[p], em))
        Ts = emit_quant_pair(sched, specs)
        for o in (0, 1):
            chain2[ps[o]] = Ts[o]
            wd_T[i + o] = Ts[o]
            wd_store[i + o] = sched.dma(
                "sp", wd_jobs[i + o][1], hq_t[:, o * 1408:(o + 1) * 1408],
                deps=(Ts[o]["q"],))
            range_prev[o] = wd_store[i + o]

    wd_next = [0]
    for t in range(NT):
        emit_ffn_tile(t)
        emit_hquant(t)
        if t >= 8 and wd_next[0] < 16:
            emit_wdpair(wd_next[0])
            wd_next[0] += 2
    assert wd_next[0] == 16

    wdsts = tuple(wd_store)
    ffa_done = tuple(h_parts_of[NT - 1])
    t_hqT = []
    for half in range(2):
        t_hqT.append([sched.dma(
            "sp", hqT[:, ic * N + half * 1024: ic * N + half * 1024 + 1024],
            hq_dram[half * 1024:(half + 1) * 1024, ic * 128:(ic + 1) * 128],
            transpose=True,
            deps=tuple(tk for tk in t_hq_store[half * 8:(half + 1) * 8] if tk) +
            ffa_done)
            for ic in range(IC)])
    t_wdT = [sched.dma("sp", wdT[:, ic * D:(ic + 1) * D],
                       wdq_d[:, ic * 128:(ic + 1) * 128], transpose=True,
                       deps=tuple(t_hq_store) + wdsts) for ic in range(IC)]

    def mk_dmm(e, t, ps, ic0, ic1):
        last = None
        for ic in range(ic0, ic1):
            stat = hqT[:, ic * N + t * 128: ic * N + (t + 1) * 128]
            e.ldweights(stat)
            e.matmul(ps[:, 0:512], stat, wdT[:, ic * D: ic * D + 512],
                     start=(ic == 0), stop=(ic == IC - 1))
            last = e.matmul(ps[:, 512:1024], stat,
                            wdT[:, ic * D + 512:(ic + 1) * D],
                            start=(ic == 0), stop=(ic == IC - 1))
        return last

    down_evac = [None] * NT
    t_ostores = [None] * NT
    ln2_prev = None
    for t in range(NT):
        ps = att_ps[t % 4]
        deps = tuple(t_hqT[t // 8]) + tuple(t_wdT)
        if t >= 4:
            deps = deps + (down_evac[t - 4],)
        else:
            deps = deps + (psum_guard.get(t % 4),)
        if t < 4:
            hh = t // 8
            dep1 = tuple(t_hqT[hh][0:11]) + tuple(t_wdT[0:11])
            dep1 = dep1 + (psum_guard.get(t % 4),)
            sched.op("pe", lambda e, t=t, ps=ps: mk_dmm(e, t, ps, 0, 11),
                     deps=dep1)
            t_mm = sched.op("pe", lambda e, t=t, ps=ps: mk_dmm(e, t, ps, 11, IC),
                            deps=tuple(t_hqT[hh][11:]) + tuple(t_wdT[11:]))
        else:
            t_mm = sched.op("pe", lambda e, t=t, ps=ps: mk_dmm(e, t, ps, 0, IC),
                            deps=deps)
        yslot = y1r[:, (t % 3) * D:(t % 3 + 1) * D]
        ydeps = (t_hq_store[NT - 1], wd_store[14], wd_store[15])
        if t >= 3:
            ydeps = ydeps + (down_evac[t - 3],)
        t_yl = sched.dma("sp", yslot, y1_dram[t * 128:(t + 1) * 128, :],
                         deps=ydeps)
        oslot = outb[:, (t % 3) * D:(t % 3 + 1) * D]
        odeps = (t_mm, t_yl, t_hq_store[NT - 1], wd_store[15])
        if t >= 3:
            odeps = odeps + (t_ostores[t - 3],)
        t_ev = sched.op("dve",
                        lambda e, ps=ps, yslot=yslot, oslot=oslot: e.tensor_tensor(
                            oslot, ps[:, :], yslot, ALU.add), deps=odeps)
        down_evac[t] = t_ev
        t_bn0 = sched.op("dve", lambda e, oslot=oslot: e.bn_stats(
            bnbuf[:, 0:6], oslot[:, 0:512]), deps=(t_ev, ln2_prev))
        t_bn = sched.op("dve", lambda e, oslot=oslot: e.bn_stats(
            bnbuf[:, 6:12], oslot[:, 512:1024]), deps=(t_bn0,))
        t_ag = sched.op("dve", lambda e: e.bn_aggr(mvst[:, 2:4], bnbuf[:]),
                        deps=(t_bn,))
        t_sq = sched.op("act", lambda e: e.activation(
            sqc[:], mvst[:, 3:4], AF.Sqrt, bias=epsc[:, 0:1]), deps=(t_ag,))
        t_rsd = sched.op("dve", lambda e: e.reciprocal(rstdc[:], sqc[:]),
                         deps=(t_sq,))
        t_nm0 = sched.op("dve", lambda e: e.tensor_scalar(
            nmr[:], mvst[:, 2:3], rstdc[:, 0:1], -1.0, ALU.mult, ALU.mult),
            deps=(t_rsd,))
        t_nm = sched.op("act", lambda e, oslot=oslot: e.activation(
            oslot, oslot, AF.Identity, bias=nmr[:, 0:1], scale=rstdc[:, 0:1]),
            deps=(t_nm0, t_sq))
        t_ostores[t] = sched.dma("sp", out_d[t * 128:(t + 1) * 128, :],
                                 oslot, deps=(t_nm,))
        ln2_prev = t_nm

    finals = sched.all_dma_barrier("sp")
    sched.finalize()
    with es, nc.Block() as block:
        @block.sync
        def _(sync):
            sched.replay("sp", sync)
            for (sname, v) in finals:
                sync.wait_ge(sched.sems[sname], v)

        @block.vector
        def _(v):
            sched.replay("dve", v)

        @block.scalar
        def _(s):
            sched.replay("act", s)

        @block.gpsimd
        def _(g):
            sched.replay("pool", g)

        @block.tensor
        def _(t):
            sched.replay("pe", t)
    return nc


_NC = None


def kernel(L0_attn_input_data, L0_attn_W1, L0_attn_W2, L0_attn_bias,
           L0_attn_embeddings_table, L0_ffn_w_gate, L0_ffn_w_up,
           L0_ffn_w_down):
    global _NC
    x = np.ascontiguousarray(np.asarray(L0_attn_input_data, dtype=np.float32))
    W1 = np.ascontiguousarray(np.asarray(L0_attn_W1, dtype=np.float32))
    W2 = np.ascontiguousarray(np.asarray(L0_attn_W2, dtype=np.float32))
    bias = np.ascontiguousarray(
        np.asarray(L0_attn_bias, dtype=np.float32)).reshape(1, D)
    et = np.ascontiguousarray(
        np.asarray(L0_attn_embeddings_table, dtype=np.float32))
    wg = np.ascontiguousarray(np.asarray(L0_ffn_w_gate, dtype=np.float32))
    wu = np.ascontiguousarray(np.asarray(L0_ffn_w_up, dtype=np.float32))
    wd = np.ascontiguousarray(np.asarray(L0_ffn_w_down, dtype=np.float32))

    if _NC is None:
        _NC = build_fused()
    ident = np.eye(21, dtype=np.float32)
    eye128 = np.eye(128, dtype=np.float32)
    ins = [{"x": x[c], "w1": W1, "w2": W2, "bias": bias, "et": et,
            "ident": ident, "eye128": eye128,
            "wg": wg, "wu": wu, "wd": wd} for c in range(NCORES)]
    res = run_bass_kernel_spmd(_NC, ins, core_ids=list(range(NCORES)))
    out = np.stack([res.results[c]["out"] for c in range(NCORES)], axis=0)
    return out.astype(np.float32)
